# version 24
# baseline (speedup 1.0000x reference)
import numpy as np
from contextlib import ExitStack

import concourse.bass as bass
import concourse.mybir as mybir
from concourse.bass_utils import run_bass_kernel_spmd

F32 = mybir.dt.float32
BF16 = mybir.dt.bfloat16
AF = mybir.ActivationFunctionType
ALU = mybir.AluOpType

B, S, E, H = 2, 4096, 2048, 8192
T = 1024
EC = 16
TC = 32
HC = 64
NCORE = 8
HS = float(128) ** -0.5
CEXP = 12.0
NW = float(E * H)
EPS_T = 1e-8
EPS_LN = 1e-5

KV_GROUPS = [[0, 1, 2, 3], [4, 5, 6, 7]]
ALL_GROUP = [list(range(8))]

SEMS = ["qX", "qR", "qB", "qXB", "qK0", "qK1", "qV0", "qV1", "qU", "qGt",
        "qD0", "qD1", "qW0", "qW1", "qKS0", "qKS1", "qVS0", "qVS1",
        "qP0", "qP1", "qPA0", "qPA1", "qPA2", "qPA3", "qPA4", "qPA5", "qPA6", "qPA7", "qQSu", "qQSg", "qQSd", "qOS",
        "tp", "vp", "ap",
        "cc0", "cc1", "cc2", "cc3", "cc4", "cc5", "cc6"]


class _Ctx:

    def __init__(self, prev, handles):
        self.real = prev is not None
        self.prev = prev or {}
        self.h = handles
        self.counts = {s: 0 for s in SEMS}
        self.events = {}
        self.waited = {}

    def tick(self, instr, sem, n, ev=None):
        self.counts[sem] += n
        if self.real:
            instr.then_inc(self.h[sem], n)
        if ev is not None:
            assert ev not in self.events, f"duplicate event {ev}"
            self.events[ev] = (sem, self.counts[sem])
            if self.real:
                assert self.prev[ev] == self.events[ev], (
                    f"pass drift at {ev}: {self.prev[ev]} vs {self.events[ev]}")
        return instr

    def ev(self, ev, sem):
        assert ev not in self.events, f"duplicate event {ev}"
        self.events[ev] = (sem, self.counts[sem])
        if self.real:
            assert self.prev[ev] == self.events[ev]

    def wait(self, eng, engname, *evs):
        if not self.real:
            return
        for evname in evs:
            if evname is None:
                continue
            if evname not in self.prev:
                assert "-" in evname or _PHASE < 99, f"unknown event {evname}"
                continue
            sem, val = self.prev[evname]
            if val <= 0:
                continue
            key = (engname, sem)
            if self.waited.get(key, -1) >= val:
                continue
            self.waited[key] = val
            eng.wait_ge(self.h[sem], val)

    def wait_total(self, eng, engname, sem):
        val = self.counts[sem]
        if not self.real or val <= 0:
            return
        key = (engname, sem)
        if self.waited.get(key, -1) >= val:
            return
        self.waited[key] = val
        eng.wait_ge(self.h[sem], val)


_PHASE = 99


def _build(prev=None):
    nc = bass.Bass("TRN2", target_bir_lowering=False,
                   detect_race_conditions=False)
    dt_in = {}

    def din(name, shape):
        dt_in[name] = nc.dram_tensor(name, shape, F32, kind="ExternalInput").ap()
        return dt_in[name]

    xT = din("xT", [E, T])
    qwT = din("qwT", [E, E])
    kvwT = din("kvwT", [E, 2 * E])
    owT = din("owT", [E, E])
    qb = din("qb", [1, E])
    kb = din("kb", [1, E])
    vb = din("vb", [1, E])
    upT_s = din("upT_s", [256, H])
    gateT_s = din("gateT_s", [256, H])
    downT_s = din("downT_s", [16, 128, 1024])
    cos_hi = din("cos_hi", [128, T])
    cos_lo = din("cos_lo", [128, T])
    sin_hi = din("sin_hi", [128, T])
    sin_lo = din("sin_lo", [128, T])
    rmat = din("rmat", [128, 128])
    ident = din("ident", [128, 128])

    out_s = nc.dram_tensor("out", [T, E], F32, kind="ExternalOutput").ap()
    out_dbg = nc.dram_tensor("out_dbg", [128, 16384], F32, kind="ExternalOutput").ap()

    k_send = nc.dram_tensor("k_send", [E, T], BF16).ap()
    v_send = nc.dram_tensor("v_send", [T, E], BF16).ap()
    k_full = nc.dram_tensor("k_full", [4, EC, 128, T], BF16).ap()
    v_full = nc.dram_tensor("v_full", [16, 2, 128, E], BF16).ap()
    up_send = nc.dram_tensor("up_send", [256, H], BF16).ap()
    gate_send = nc.dram_tensor("gate_send", [256, H], BF16).ap()
    down_send = nc.dram_tensor("down_send", [16, 128, 1024], BF16).ap()
    up_full = nc.dram_tensor("up_full", [8, 2, 128, H], BF16).ap()
    gate_full = nc.dram_tensor("gate_full", [8, 2, 128, H], BF16).ap()
    down_full = nc.dram_tensor("down_full", [8, 16, 128, 1024], BF16).ap()
    scl_send = nc.dram_tensor("scl_send", [1, 4], F32).ap()
    scl_full = nc.dram_tensor("scl_full", [1, 4], F32).ap()
    warm_a = nc.dram_tensor("warm_a", [1, 4], F32).ap()
    warm_b = nc.dram_tensor("warm_b", [8, 4], F32).ap()

    with ExitStack() as ctx:
        sb = lambda n, s, dt: ctx.enter_context(nc.sbuf_tensor(n, s, dt))
        A_x = sb("A_x", [128, 16384], F32)
        A_xb = sb("A_xb", [128, 16384], BF16)
        A_q = sb("A_q", [128, 16384], BF16)
        A_exp = sb("A_exp", [128, 16384], BF16)
        A_kv = sb("A_kv", [128, 4096], BF16)
        A_wt = sb("A_wt", [128, 8192], BF16)
        A_rp = sb("A_rp", [128, 2048], BF16)
        A_sc = sb("A_sc", [128, 2048], F32)
        s_qt = sb("s_qt", [128, 1024], BF16)
        s_sm = sb("s_sm", [128, 648], BF16)
        s_mf = sb("s_mf", [128, 2048], F32)
        s_acc = sb("s_acc", [128, 52], F32)
        pAll = ctx.enter_context(nc.psum_tensor("pAll", [128, 4096], F32))

        handles = {s: ctx.enter_context(nc.semaphore(s)) for s in SEMS} \
            if prev is not None else {s: None for s in SEMS}
        C = _Ctx(prev, handles)
        block = ctx.enter_context(nc.Block())

        bank = lambda i, w=512: pAll[:, i * 512:i * 512 + w]

        def xs(ec, lo, n):
            return A_x[:, ec * 1024 + lo: ec * 1024 + lo + n]

        def xbs(ec, lo, n):
            return A_xb[:, ec * 1024 + lo: ec * 1024 + lo + n]

        def qslice(ec, lo, n):
            return A_q[:, ec * 1024 + lo: ec * 1024 + lo + n]

        ones512_b = s_sm[0:1, 0:512]
        ones128r_b = s_sm[0:1, 512:640]
        onescol_b = s_sm[:, 640:641]
        onescol_f = s_mf[:, 0:1]
        ones1_f = s_mf[0:1, 1:2]
        ones128r_f = s_mf[0:1, 2:130]
        scol = lambda j: s_mf[:, 130 + j:131 + j]
        srow = s_mf[0:1, 162:166]
        sclback_row = s_mf[0:1, 166:170]
        drow = s_mf[0:1, 150:158]
        rowA = s_mf[0:1, 178:690]
        recB = s_mf[:, 690:1202]
        rowB = s_mf[0:1, 690:1202]
        identsl = s_mf[:, 1202:1330]
        rowC = s_mf[0:1, 1330:1842]
        ln2col = lambda j: s_mf[:, 1842 + j:1843 + j]
        rmatsl = s_mf[:, 1858:1986]
        c_negC = s_mf[:, 1986:1987]
        c_eps = s_mf[:, 1987:1988]

        qb_b = lambda lo, n: A_exp[0:1, lo:lo + n]
        kb_b = lambda lo, n: A_exp[0:1, 2048 + lo:2048 + lo + n]
        vb_b = lambda lo, n: A_exp[0:1, 4096 + lo:4096 + lo + n]

        cos_sl = lambda tg: A_rp[:, tg * 512:(tg + 1) * 512]
        sin_sl = lambda tg: A_rp[:, 1024 + tg * 512:1024 + tg * 512 + 512]

        def kq_groups(nslab):
            for s in range(nslab):
                for ci in range(2):
                    for tg in range(2):
                        yield s, 2 * s + ci, tg, s * 4 + ci * 2 + tg

        @block.sync
        def _(sync):
            W = lambda *e: C.wait(sync, "sync", *e)
            TK = lambda i, sem, n, ev=None: C.tick(i, sem, n, ev)

            TK(sync.dma_start(out=identsl, in_=ident), "qX", 16)
            TK(sync.dma_start(out=rmatsl, in_=rmat), "qX", 16)
            TK(sync.dma_start(
                out=A_x.ap().rearrange("p (ec t) -> p ec t", t=1024),
                in_=xT.rearrange("(ec p) t -> p ec t", p=128)), "qX", 16, "xload")
            TK(sync.dma_start(out=A_sc[:, 0:1024], in_=cos_hi), "qX", 16)
            TK(sync.dma_start(out=A_sc[:, 1024:2048], in_=cos_lo), "qX", 16, "coshl")
            W("cosdone")
            TK(sync.dma_start(out=A_sc[:, 0:1024], in_=sin_hi), "qX", 16)
            TK(sync.dma_start(out=A_sc[:, 1024:2048], in_=sin_lo), "qX", 16, "sinhl")
            C.ev("p0done", "qX")

            if _PHASE < 3:
                return
            for qg in range(2 if _PHASE >= 3.9 else 1):
                W("cc_k")
                W("qlastmm")
                for ks in range(16):
                    a = qg * 16 + ks - 2
                    W(f"scoreslab_{a // 16}_{a % 16}" if a >= 0 else None)
                    rank, tl0 = ks // 4, (ks % 4) * 256
                    TK(sync.dma_start(
                        out=A_xb.ap()[:, (ks % 2) * 4096:(ks % 2) * 4096 + 4096]
                            .rearrange("p (ec t) -> p ec t", t=256),
                        in_=k_full[rank][:, :, tl0:tl0 + 256]
                            .transpose([1, 0, 2])),
                       f"qK{ks % 2}", 16, f"kload_{qg}_{ks}")
                if _PHASE < 3.2:
                    continue
                W("cc_v")
                for eh in range(2):
                    for vs in range(16):
                        a = (qg * 2 + eh) * 16 + vs - 2
                        W(f"avslab_{a // 32}_{(a % 32) // 16}_{a % 16}" if a >= 0 else None)
                        TK(sync.dma_start(
                            out=A_kv.ap()[:, (vs % 2) * 2048:(vs % 2) * 2048 + 2048]
                                .rearrange("p (c e) -> p c e", e=1024),
                            in_=v_full[vs][:, :, eh * 1024:(eh + 1) * 1024]
                                .transpose([1, 0, 2])),
                           f"qV{vs % 2}", 16, f"vload_{qg}_{eh}_{vs}")

            if _PHASE < 5:
                return
            for tg in range(2):
                W("cc_up")
                W("cc_gate")
                W("x2done")
                if tg == 1:
                    W("downmm_0_15_1")
                for j in range(32):
                    W(f"upmm_{tg}_{j - 1}")
                    for rl in range(2):
                        TK(sync.dma_start(
                            out=A_wt.ap()[:, 0:4096]
                                .rearrange("p (r rl c) -> p r rl c", rl=2, c=256)[:, :, rl, :],
                            in_=up_full[:, rl, :, j * 256:(j + 1) * 256]
                                .rearrange("r p c -> p r c")),
                           "qU", 16, f"uload_{tg}_{j}" if rl == 1 else None)
                    W(f"gatemm_{tg}_{j - 1}")
                    for rl in range(2):
                        TK(sync.dma_start(
                            out=A_wt.ap()[:, 4096:8192]
                                .rearrange("p (r rl c) -> p r rl c", rl=2, c=256)[:, :, rl, :],
                            in_=gate_full[:, rl, :, j * 256:(j + 1) * 256]
                                .rearrange("r p c -> p r c")),
                           "qGt", 16, f"gload_{tg}_{j}" if rl == 1 else None)
                W("cc_down")
                for oc in range(16):
                    if oc == 0:
                        W(f"upmm_{tg}_31")
                    else:
                        W(f"downmm_{tg}_{oc - 1}_0")
                    TK(sync.dma_start(
                        out=A_wt.ap()[:, 0:4096]
                            .rearrange("p (r f) -> p r f", f=1024),
                        in_=down_full[0:4, oc, :, :]
                            .rearrange("r p f -> p r f")),
                       "qD0", 16, f"dload_{tg}_{oc}_0")

        @block.scalar
        def _(scalar):
            W = lambda *e: C.wait(scalar, "scalar", *e)
            TK = lambda i, sem, n, ev=None: C.tick(i, sem, n, ev)
            for s, co, tg, g in kq_groups(8):
                W(f"kproj_{g}", f"krope_{g - 2}")
                TK(scalar.activation(out=A_sc[:, (g % 2) * 512:(g % 2) * 512 + 512],
                                     in_=bank(g % 2), func=AF.Copy),
                   "ap", 1, f"kevac_{g}")
            for vg in range(64):
                W(f"vevac_{vg}")
                s, tc = vg // 8, vg % 8
                TK(scalar.dma_start(
                    out=v_send[tc * 128:(tc + 1) * 128, s * 256:(s + 1) * 256],
                    in_=A_kv[:, 1024 + (vg % 2) * 256:1024 + (vg % 2) * 256 + 256]),
                   f"qVS{vg % 2}", 16, f"vstore_{vg}")
            for s, co, tg, g in kq_groups(8):
                W(f"qproj_{g}", f"qrope_{g - 2}")
                TK(scalar.activation(out=A_sc[:, (g % 2) * 512:(g % 2) * 512 + 512],
                                     in_=bank(g % 2), func=AF.Copy, scale=HS),
                   "ap", 1, f"qevac_{g}")
            if _PHASE < 2:
                return
            def tile_in(i):
                r, j = i // 16, i % 16
                src = (upT_s, gateT_s)[r] if r < 2 else None
                if r < 2:
                    rr, cc2 = j // 8, j % 8
                    return src[rr * 128:(rr + 1) * 128, cc2 * 1024:(cc2 + 1) * 1024]
                return downT_s[j]

            def tile_out(i):
                r, j = i // 16, i % 16
                dst = (up_send, gate_send)[r] if r < 2 else None
                if r < 2:
                    rr, cc2 = j // 8, j % 8
                    return dst[rr * 128:(rr + 1) * 128, cc2 * 1024:(cc2 + 1) * 1024]
                return down_send[j]

            W("qdone")
            W("sclrow")
            TK(scalar.dma_start(out=scl_send, in_=srow), "qB", 16, "sclstore")
            W("cc_scl")
            TK(scalar.dma_start(out=sclback_row, in_=scl_full), "qB", 16, "sclload")
            qacts = []
            for _i in range(48):
                qacts.append(("L", _i))
                if _i >= 2:
                    qacts.append(("S", _i - 2))
            qacts += [("S", 46), ("S", 47)]
            qpos = [0]

            def quant_dmas(n):
                for _ in range(n):
                    if qpos[0] >= len(qacts):
                        return
                    kind, i = qacts[qpos[0]]
                    qpos[0] += 1
                    if kind == "L":
                        W(f"quant_{i - 2}")
                        TK(scalar.dma_start(
                            out=A_sc[:, (i % 2) * 1024:(i % 2) * 1024 + 1024],
                            in_=tile_in(i)), f"qP{i % 2}", 16, f"qtload_{i}")
                    else:
                        W(f"quant_{i}")
                        TK(scalar.dma_start(out=tile_out(i), in_=s_qt[:, 0:1024]),
                           ("qQSu", "qQSg", "qQSd")[i // 16], 16, f"qtstore_{i}")

            quant_dmas(len(qacts))

            if _PHASE < 3:
                return
            if _PHASE == 3.06:
                W("score_0_0")
                TK(scalar.activation(out=A_sc[:, 0:512], in_=bank(0), func=AF.Copy),
                   "ap", 1, "dbg1")
                W("score_0_1")
                TK(scalar.activation(out=A_sc[:, 512:1024], in_=bank(1), func=AF.Copy),
                   "ap", 1, "dbg2")
            for qg in range(2 if _PHASE >= 3.9 else 1):
                for tc in range((4 if _PHASE == 3.066 else TC) if _PHASE >= 3.065 else 0):
                    W(f"score_{qg}_{tc}")
                    if qg == 1 and tc == 0:
                        W("avdone_0_1")
                    if _PHASE == 3.065:
                        TK(scalar.activation(out=A_exp[:, tc * 512:(tc + 1) * 512],
                                             in_=bank(tc % 2), func=AF.Copy),
                           "ap", 1, f"exp_{qg}_{tc}")
                    else:
                        TK(scalar.activation(out=A_exp[:, tc * 512:(tc + 1) * 512],
                                             in_=bank(tc % 2), func=AF.Exp,
                                             bias=c_negC),
                           "ap", 1, f"exp_{qg}_{tc}")
                if _PHASE >= 3.09:
                    W(f"summm_{qg}_{TC - 1}")
                    TK(scalar.activation(out=rowA, in_=bank(2)[0:1, :], func=AF.Copy),
                       "ap", 1, f"sumevac_{qg}")
            if _PHASE < 4:
                return
            W("oadd_1_15")
            for tg in range(2):
                for ec in range(EC):
                    W(f"sqmm_0_{tg}_{ec - 2}")
                    TK(scalar.activation(out=A_rp[:, (ec % 2) * 512:(ec % 2) * 512 + 512],
                                         in_=xs(ec, tg * 512, 512), func=AF.Square),
                       "ap", 1, f"sq_0_{tg}_{ec}")
                W(f"meanmm_0_{tg}")
                TK(scalar.activation(out=rowA, in_=bank(2)[0:1, :], func=AF.Copy,
                                     scale=1.0 / E), "ap", 1, f"mrow_0_{tg}")
                W(f"sqmm_0_{tg}_{EC - 1}")
                TK(scalar.activation(out=rowB, in_=bank(3)[0:1, :], func=AF.Copy,
                                     scale=1.0 / E), "ap", 1, f"sqrow_0_{tg}")
                W(f"var_0_{tg}")
                TK(scalar.activation(out=rowC, in_=rowB, func=AF.Sqrt,
                                     bias=c_eps[0:1, :]), "ap", 1, f"sqrt_0_{tg}")

            if _PHASE < 5:
                return
            for tg in range(2):
                W("x2done", f"ostore_{tg - 1}_3")
                for hc in range(HC):
                    W(f"upg_{tg}_{hc}", f"hid_{tg}_{hc - 2}")
                    TK(scalar.activation(out=A_sc[:, (hc % 2) * 512:(hc % 2) * 512 + 512],
                                         in_=bank(hc % 2), func=AF.Silu,
                                         scale=scol(0)), "ap", 1, f"silu_{tg}_{hc}")
                for ec in range(EC):
                    W(f"x3_{tg}_{ec}")
                    W(f"sqmm_1_{tg}_{ec - 2}")
                    TK(scalar.activation(out=A_rp[:, (ec % 2) * 512:(ec % 2) * 512 + 512],
                                         in_=xs(ec, tg * 512, 512), func=AF.Square),
                       "ap", 1, f"sq_1_{tg}_{ec}")
                W(f"meanmm_1_{tg}")
                TK(scalar.activation(out=rowA, in_=bank(6)[0:1, :], func=AF.Copy,
                                     scale=1.0 / E), "ap", 1, f"mrow_1_{tg}")
                W(f"sqmm_1_{tg}_{EC - 1}")
                TK(scalar.activation(out=rowB, in_=bank(7)[0:1, :], func=AF.Copy,
                                     scale=1.0 / E), "ap", 1, f"sqrow_1_{tg}")
                W(f"var_1_{tg}")
                TK(scalar.activation(out=rowC, in_=rowB, func=AF.Sqrt,
                                     bias=c_eps[0:1, :]), "ap", 1, f"sqrt_1_{tg}")

        @block.vector
        def _(vector):
            W = lambda *e: C.wait(vector, "vector", *e)
            TK = lambda i, sem, n, ev=None: C.tick(i, sem, n, ev)
            W("coshl")
            TK(vector.tensor_add(A_rp[:, 0:1024], A_sc[:, 0:1024], A_sc[:, 1024:2048]),
               "vp", 1, "cosdone")
            W("sinhl")
            TK(vector.tensor_add(A_rp[:, 1024:2048], A_sc[:, 0:1024], A_sc[:, 1024:2048]),
               "vp", 1, "sindone")
            vector.memset(ones512_b, 1.0)
            vector.memset(ones128r_b, 1.0)
            vector.memset(onescol_b, 1.0)
            vector.memset(onescol_f, 1.0)
            vector.memset(ones1_f, 1.0)
            vector.memset(srow, 0.0)
            vector.memset(c_negC, -CEXP)
            vector.memset(c_eps, EPS_LN)
            TK(vector.memset(ones128r_f, 1.0), "vp", 1, "ones")
            p1i = [0]

            def p1slot():
                if _PHASE < 2 or p1i[0] >= 48:
                    return
                i = p1i[0]
                p1i[0] += 1
                W(f"p1load_{i}")
                TK(vector.tensor_reduce(
                    out=s_acc[:, i:i + 1],
                    in_=A_exp[:, 6656 + (i % 8) * 1024:6656 + (i % 8) * 1024 + 1024],
                    axis=mybir.AxisListType.X, op=ALU.add,
                    apply_absolute_value=True), "vp", 1, f"p1_{i}")

            def p1drain(n):
                for _ in range(n):
                    if p1i[0] >= 48:
                        return
                    p1slot()

            for s, co, tg, g in kq_groups(8):
                W(f"rotmm_K_{g}")
                TK(vector.scalar_tensor_tensor(
                    out=A_sc[:, 1024 + (g % 2) * 512:1024 + (g % 2) * 512 + 512],
                    in0=bank(2 + g % 2), scalar=1.0, in1=sin_sl(tg),
                    op0=ALU.mult, op1=ALU.mult), "vp", 1, f"kt2_{g}")
                ktmp = A_sc[:, (g % 2) * 512:(g % 2) * 512 + 512]
                vector.tensor_mul(ktmp, ktmp, cos_sl(tg))
                W(f"kstore_{g - 2}")
                TK(vector.tensor_add(
                    A_kv[:, (g % 2) * 512:(g % 2) * 512 + 512], ktmp,
                    A_sc[:, 1024 + (g % 2) * 512:1024 + (g % 2) * 512 + 512]),
                   "vp", 1, f"krope_{g}")
                if g % 2 == 1:
                    p1slot()
            for vg in range(64):
                W(f"vproj_{vg}", f"vstore_{vg - 2}")
                TK(vector.tensor_copy(
                    out=A_kv[:, 1024 + (vg % 2) * 256:1024 + (vg % 2) * 256 + 256],
                    in_=bank(vg % 2, 256)), "vp", 1, f"vevac_{vg}")
                if vg % 4 == 0:
                    p1slot()
            for s, co, tg, g in kq_groups(8):
                W(f"rotmm_Q_{g}")
                TK(vector.scalar_tensor_tensor(
                    out=A_sc[:, 1024 + (g % 2) * 512:1024 + (g % 2) * 512 + 512],
                    in0=bank(2 + g % 2), scalar=1.0, in1=sin_sl(tg),
                    op0=ALU.mult, op1=ALU.mult), "vp", 1, f"qt2_{g}")
                qtmp = A_sc[:, (g % 2) * 512:(g % 2) * 512 + 512]
                vector.tensor_mul(qtmp, qtmp, cos_sl(tg))
                TK(vector.tensor_add(
                    qslice(co, tg * 512, 512), qtmp,
                    A_sc[:, 1024 + (g % 2) * 512:1024 + (g % 2) * 512 + 512]),
                   "vp", 1, f"qrope_{g}")
                if g % 4 == 3:
                    p1slot()
                if g == 31:
                    C.ev("qdone", "vp")
            if _PHASE < 2:
                return
            p1drain(48)
            for r in range(3):
                TK(vector.reduce_sum(out=s_acc[:, 48 + r:49 + r],
                                     in_=s_acc[:, r * 16:(r + 1) * 16],
                                     axis=mybir.AxisListType.X),
                   "vp", 1, "p1red" if r == 2 else None)
            W("sclsum")
            TK(vector.tensor_copy(out=srow[0:1, 0:3], in_=bank(3)[0:1, 0:3]),
               "vp", 1, "sclrow")
            W("thrbcast")
            TK(vector.tensor_scalar(out=s_mf[:, 132:135], in0=bank(3)[:, 0:3],
                                    scalar1=0.5 / NW, scalar2=0.5 * EPS_T,
                                    op0=ALU.mult, op1=ALU.add), "vp", 1)
            TK(vector.tensor_scalar(out=s_mf[:, 135:138], in0=bank(3)[:, 0:3],
                                    scalar1=-0.5 / NW, scalar2=-0.5 * EPS_T,
                                    op0=ALU.mult, op1=ALU.add), "vp", 1)
            TK(vector.tensor_scalar(out=s_mf[:, 130:131], in0=bank(3)[:, 0:1],
                                    scalar1=-1.0 / NW, scalar2=-EPS_T,
                                    op0=ALU.mult, op1=ALU.add), "vp", 1)
            TK(vector.tensor_scalar(out=s_mf[:, 131:132], in0=bank(3)[:, 1:2],
                                    scalar1=1.0 / NW, scalar2=EPS_T,
                                    op0=ALU.mult, op1=ALU.add), "vp", 1)
            TK(vector.tensor_scalar(out=s_mf[:, 138:139], in0=bank(3)[:, 2:3],
                                    scalar1=1.0 / NW, scalar2=EPS_T,
                                    op0=ALU.mult, op1=ALU.add),
               "vp", 1, "scols")
            vqi = [0]

            def vquant(n):
                for _ in range(n):
                    if vqi[0] >= 48:
                        return
                    i = vqi[0]
                    vqi[0] += 1
                    r = i // 16
                    wtile = A_sc[:, (i % 2) * 1024:(i % 2) * 1024 + 1024]
                    W(f"qtload_{i}", f"qtstore_{i - 1}")
                    TK(vector.tensor_single_scalar(out=s_qt[:, 0:1024], in_=wtile,
                                                   scalar=scol(2 + r), op=ALU.is_gt),
                       "vp", 1)
                    TK(vector.scalar_tensor_tensor(out=s_qt[:, 0:1024], in0=wtile,
                                                   scalar=scol(5 + r), in1=s_qt[:, 0:1024],
                                                   op0=ALU.is_lt, op1=ALU.subtract),
                       "vp", 1, f"quant_{i}")

            vquant(48)
            if _PHASE < 3:
                return
            for qg in range(2 if _PHASE >= 3.9 else 1):
                if _PHASE < 3.2:
                    continue
                W(f"sumevac_{qg}")
                TK(vector.reciprocal(out=rowA, in_=rowA), "vp", 1, f"recip_{qg}")
                W(f"recipbcmm_{qg}")
                TK(vector.tensor_copy(out=recB, in_=bank(0)), "vp", 1, f"recB_{qg}")
                for eh in range(2):
                    W(f"avdone_{qg}_{eh}")
                    if qg == 1:
                        W("oproj_0_15")
                    for e8 in range(8):
                        TK(vector.scalar_tensor_tensor(
                            out=A_xb[:, 8192 + (eh * 8 + e8) * 512:
                                     8192 + (eh * 8 + e8) * 512 + 512],
                            in0=bank(e8), scalar=1.0, in1=recB,
                            op0=ALU.mult, op1=ALU.mult),
                           "vp", 1, f"avevac_{qg}_{eh}_{e8}")
                for oc in range(EC if _PHASE >= 3.3 else 0):
                    W(f"oproj_{qg}_{oc}", "p0done")
                    TK(vector.tensor_add(xs(oc, qg * 512, 512), bank(oc % 2),
                                         xs(oc, qg * 512, 512)),
                       "vp", 1, f"oadd_{qg}_{oc}")
            if _PHASE < 4:
                return
            for tg in range(2):
                W(f"mrow_0_{tg}", f"sqrow_0_{tg}")
                TK(vector.tensor_mul(rowC, rowA, rowA), "vp", 1)
                TK(vector.tensor_sub(rowB, rowB, rowC), "vp", 1, f"var_0_{tg}")
                W(f"sqrt_0_{tg}")
                TK(vector.reciprocal(out=rowC, in_=rowC), "vp", 1, f"rrow_0_{tg}")
                W(f"bcastm_0_{tg}", "quant_47")
                TK(vector.tensor_copy(out=A_sc[:, tg * 512:tg * 512 + 512], in_=bank(2)),
                   "vp", 1, f"mb_{tg}")
                W(f"bcastr_0_{tg}")
                TK(vector.tensor_copy(out=A_sc[:, 1024 + tg * 512:1024 + tg * 512 + 512],
                                      in_=bank(3)), "vp", 1, f"rb_{tg}")
            for ec in range(EC):
                x_ec = xs(ec, 0, 1024)
                TK(vector.tensor_sub(x_ec, x_ec, A_sc[:, 0:1024]), "vp", 1)
                TK(vector.tensor_mul(x_ec, x_ec, A_sc[:, 1024:2048]), "vp", 1)
                TK(vector.tensor_copy(out=xbs(ec, 0, 1024), in_=x_ec),
                   "vp", 1, "x2done" if ec == EC - 1 else f"ln1_{ec}")
            if _PHASE < 5:
                return
            for tg in range(2):
                for hc in range(HC):
                    hid = (A_q if hc < 32 else A_exp)[:, (hc % 32) * 512:(hc % 32) * 512 + 512]
                    W(f"silu_{tg}_{hc}", f"gmm_{tg}_{hc}")
                    TK(vector.scalar_tensor_tensor(
                        out=hid, in0=bank(2 + hc % 2), scalar=scol(1),
                        in1=A_sc[:, (hc % 2) * 512:(hc % 2) * 512 + 512],
                        op0=ALU.mult, op1=ALU.mult), "vp", 1, f"hid_{tg}_{hc}")
                for oc in range(EC):
                    W(f"downmm_{tg}_{oc}_1")
                    TK(vector.scalar_tensor_tensor(
                        out=xs(oc, tg * 512, 512), in0=bank(4 + oc % 2),
                        scalar=scol(8), in1=xs(oc, tg * 512, 512),
                        op0=ALU.mult, op1=ALU.add), "vp", 1, f"x3_{tg}_{oc}")
                W(f"mrow_1_{tg}", f"sqrow_1_{tg}")
                TK(vector.tensor_mul(rowC, rowA, rowA), "vp", 1)
                TK(vector.tensor_sub(rowB, rowB, rowC), "vp", 1, f"var_1_{tg}")
                W(f"sqrt_1_{tg}")
                TK(vector.reciprocal(out=rowC, in_=rowC), "vp", 1, f"rrow_1_{tg}")
                TK(vector.tensor_mul(rowA, rowA, rowC), "vp", 1)
                TK(vector.tensor_scalar_mul(rowA, rowA, -1.0), "vp", 1, f"nmr_{tg}")
                for tsub in range(4):
                    W(f"colmm_{tg}_{tsub}")
                    TK(vector.tensor_copy(out=ln2col(tsub), in_=bank(6)[:, tsub:tsub + 1]),
                       "vp", 1)
                    TK(vector.tensor_copy(out=ln2col(4 + tsub), in_=bank(7)[:, tsub:tsub + 1]),
                       "vp", 1, f"cols_{tg}_{tsub}")
                for tsub in range(4):
                    grp = tsub % 2
                    W(f"trans_{tg}_{tsub}", f"ostore_{tg}_{tsub - 1}")
                    TK(vector.tensor_scalar(
                        out=A_sc[:, 0:2048], in0=pAll[:, grp * 2048:grp * 2048 + 2048],
                        scalar1=ln2col(tsub), scalar2=ln2col(4 + tsub),
                        op0=ALU.mult, op1=ALU.add), "vp", 1, f"ln2evac_{tg}_{tsub}")

        @block.tensor
        def _(tensor):
            W = lambda *e: C.wait(tensor, "tensor", *e)
            TK = lambda i, sem, n, ev=None: C.tick(i, sem, n, ev)
            W("ones", "xbload", "biasload", "p0done")
            wbuf = lambda b: A_wt[:, b * 4096:(b + 1) * 4096]
            for s, co, tg, g in kq_groups(8):
                W(f"kvw_{s}", f"kevac_{g - 2}")
                for ec in range(EC):
                    tensor.matmul(bank(g % 2),
                                  wbuf(s % 2)[:, ec * 256 + (co - 2 * s) * 128:
                                              ec * 256 + (co - 2 * s) * 128 + 128],
                                  xbs(ec, tg * 512, 512),
                                  start=(ec == 0), stop=False)
                TK(tensor.matmul(bank(g % 2), kb_b(co * 128, 128), ones512_b,
                                 start=False, stop=True),
                   "tp", 1, f"kproj_{g}")
                if co == 2 * s + 1 and tg == 1:
                    C.ev(f"kvslab_done_{s}", "tp")
                W(f"kevac_{g}", f"kt2_{g - 2}")
                TK(tensor.matmul(bank(2 + g % 2), rmatsl,
                                 A_sc[:, (g % 2) * 512:(g % 2) * 512 + 512],
                                 start=True, stop=True),
                   "tp", 1, f"rotmm_K_{g}")
            for vg in range(64):
                s, tc = vg // 8, vg % 8
                W(f"kvw_{8 + s}", f"vevac_{vg - 2}")
                for ec in range(EC):
                    tensor.matmul(bank(vg % 2, 256),
                                  xbs(ec, tc * 128, 128),
                                  wbuf((8 + s) % 2)[:, ec * 256:ec * 256 + 256],
                                  start=(ec == 0), stop=False)
                TK(tensor.matmul(bank(vg % 2, 256), ones128r_b, vb_b(s * 256, 256),
                                 start=False, stop=True),
                   "tp", 1, f"vproj_{vg}")
                if tc == 7:
                    C.ev(f"vslab_done_{8 + s}", "tp")
            for s, co, tg, g in kq_groups(8):
                W(f"qw_{s}", f"qevac_{g - 2}")
                if g < 2:
                    W(f"vevac_{62 + g}")
                for ec in range(EC):
                    tensor.matmul(bank(g % 2),
                                  wbuf(s % 2)[:, ec * 256 + (co - 2 * s) * 128:
                                              ec * 256 + (co - 2 * s) * 128 + 128],
                                  xbs(ec, tg * 512, 512),
                                  start=(ec == 0), stop=False)
                TK(tensor.matmul(bank(g % 2), qb_b(co * 128, 128), ones512_b,
                                 start=False, stop=True),
                   "tp", 1, f"qproj_{g}")
                if co == 2 * s + 1 and tg == 1:
                    C.ev(f"qslab_done_{s}", "tp")
                W(f"qevac_{g}", f"qt2_{g - 2}")
                TK(tensor.matmul(bank(2 + g % 2), rmatsl,
                                 A_sc[:, (g % 2) * 512:(g % 2) * 512 + 512],
                                 start=True, stop=True),
                   "tp", 1, f"rotmm_Q_{g}")
                if g == 31:
                    C.ev("qlastmm", "tp")
            if _PHASE < 2:
                return
            W("p1red")
            TK(tensor.matmul(bank(3)[0:1, 0:3], onescol_f,
                             s_acc[:, 48:51], start=True, stop=True),
               "tp", 1, "sclsum")
            W("sclload", "sclrow")
            TK(tensor.matmul(bank(3)[:, 0:3], ones128r_f, sclback_row[0:1, 0:3],
                             start=True, stop=True), "tp", 1, "thrbcast")
            if _PHASE < 3:
                return
            for qg in range(2 if _PHASE >= 3.9 else 1):
                for ks in range(1 if _PHASE == 3.06 else (16 if _PHASE >= 3.05 else 0)):
                    for tsub in range(2):
                        tc = 2 * ks + tsub
                        if _PHASE == 3.067:
                            W(f"kload_{qg}_{ks}")
                        else:
                            W(f"kload_{qg}_{ks}", f"exp_{qg}_{tc - 2}")
                        for ec in range(EC):
                            mm = tensor.matmul(
                                bank(tc % 2),
                                A_xb[:, (ks % 2) * 4096 + ec * 256 + tsub * 128:
                                     (ks % 2) * 4096 + ec * 256 + tsub * 128 + 128],
                                qslice(ec, qg * 512, 512),
                                start=(ec == 0), stop=(ec == EC - 1))
                        TK(mm, "tp", 1, f"score_{qg}_{tc}")
                        if tsub == 1:
                            C.ev(f"scoreslab_{qg}_{ks}", "tp")
                        if tc >= 1 and _PHASE >= 3.09:
                            W(f"exp_{qg}_{tc - 1}")
                            TK(tensor.matmul(bank(2)[0:1, :], onescol_b,
                                             A_exp[:, (tc - 1) * 512:tc * 512],
                                             start=(tc - 1 == 0), stop=False),
                               "tp", 1, f"summm_{qg}_{tc - 1}")
                if _PHASE >= 3.09:
                    W(f"exp_{qg}_{TC - 1}")
                    TK(tensor.matmul(bank(2)[0:1, :], onescol_b,
                                     A_exp[:, (TC - 1) * 512:TC * 512],
                                     start=False, stop=True),
                       "tp", 1, f"summm_{qg}_{TC - 1}")
                if _PHASE < 3.2:
                    continue
                W(f"recip_{qg}")
                TK(tensor.matmul(bank(0), ones128r_f, rowA, start=True, stop=True),
                   "tp", 1, f"recipbcmm_{qg}")
                for eh in range(2):
                    for tc in range(TC):
                        vs = tc // 2
                        W(f"vload_{qg}_{eh}_{vs}")
                        if tc == 0:
                            W(f"recB_{qg}", f"sumevac_{qg}")
                            if eh == 1:
                                for e8 in range(8):
                                    W(f"avevac_{qg}_0_{e8}")
                        for e8 in range(8):
                            mm = tensor.matmul(
                                bank(e8),
                                A_kv[:, (vs % 2) * 2048 + (tc % 2) * 1024 + e8 * 128:
                                     (vs % 2) * 2048 + (tc % 2) * 1024 + e8 * 128 + 128],
                                A_exp[:, tc * 512:(tc + 1) * 512],
                                start=(tc == 0), stop=(tc == TC - 1))
                        if tc % 2 == 1:
                            TK(mm, "tp", 1, f"avslab_{qg}_{eh}_{vs}")
                    C.ev(f"avdone_{qg}_{eh}", "tp")
                for oc in range(EC if _PHASE >= 3.3 else 0):
                    W(f"owload_{qg}_{oc // 2}", f"oadd_{qg}_{oc - 2}")
                    if oc == 0:
                        for e8 in range(8):
                            W(f"avevac_{qg}_1_{e8}")
                    for ec in range(EC):
                        mm = tensor.matmul(
                            bank(oc % 2),
                            wbuf((oc // 2) % 2)[:, ec * 256 + (oc % 2) * 128:
                                                ec * 256 + (oc % 2) * 128 + 128],
                            A_xb[:, 8192 + ec * 512:8192 + ec * 512 + 512],
                            start=(ec == 0), stop=(ec == EC - 1))
                    TK(mm, "tp", 1, f"oproj_{qg}_{oc}")
                    if oc % 2 == 1:
                        C.ev(f"oproj_slabdone_{qg}_{oc // 2}", "tp")
            if _PHASE < 4:
                return
            W("oadd_1_15")
            for tg in range(2):
                if tg == 1:
                    W("mb_0", "rb_0")
                for ec in range(EC):
                    mm = tensor.matmul(bank(2)[0:1, :], onescol_f, xs(ec, tg * 512, 512),
                                       start=(ec == 0), stop=(ec == EC - 1))
                TK(mm, "tp", 1, f"meanmm_0_{tg}")
                for ec in range(EC):
                    W(f"sq_0_{tg}_{ec}")
                    TK(tensor.matmul(bank(3)[0:1, :], onescol_b,
                                     A_rp[:, (ec % 2) * 512:(ec % 2) * 512 + 512],
                                     start=(ec == 0), stop=(ec == EC - 1)),
                       "tp", 1, f"sqmm_0_{tg}_{ec}")
                W(f"mrow_0_{tg}")
                TK(tensor.matmul(bank(2), ones128r_f, rowA, start=True, stop=True),
                   "tp", 1, f"bcastm_0_{tg}")
                W(f"rrow_0_{tg}")
                TK(tensor.matmul(bank(3), ones128r_f, rowC, start=True, stop=True),
                   "tp", 1, f"bcastr_0_{tg}")
            if _PHASE < 5:
                return
            for tg in range(2):
                W("x2done")
                if tg == 1:
                    W("ln2evac_0_2", "ln2evac_0_3", "cols_0_3")
                for j in range(32):
                    for s01 in range(2):
                        hc = 2 * j + s01
                        W(f"uload_{tg}_{j}", f"silu_{tg}_{hc - 2}")
                        for ec in range(EC):
                            mm = tensor.matmul(
                                bank(hc % 2),
                                A_wt[:, (ec // 2) * 512 + (ec % 2) * 256 + s01 * 128:
                                     (ec // 2) * 512 + (ec % 2) * 256 + s01 * 128 + 128],
                                xbs(ec, tg * 512, 512),
                                start=(ec == 0), stop=(ec == EC - 1))
                        TK(mm, "tp", 1, f"upg_{tg}_{hc}")
                        if s01 == 1:
                            C.ev(f"upmm_{tg}_{j}", "tp")
                        W(f"gload_{tg}_{j}", f"hid_{tg}_{hc - 2}")
                        for ec in range(EC):
                            mm = tensor.matmul(
                                bank(2 + hc % 2),
                                A_wt[:, 4096 + (ec // 2) * 512 + (ec % 2) * 256 + s01 * 128:
                                     4096 + (ec // 2) * 512 + (ec % 2) * 256 + s01 * 128 + 128],
                                xbs(ec, tg * 512, 512),
                                start=(ec == 0), stop=(ec == EC - 1))
                        TK(mm, "tp", 1, f"gmm_{tg}_{hc}")
                        if s01 == 1:
                            C.ev(f"gatemm_{tg}_{j}", "tp")
                W(f"hid_{tg}_{HC - 1}")
                for oc in range(EC):
                    W(f"x3_{tg}_{oc - 1}")
                    for hh in range(2):
                        W(f"dload_{tg}_{oc}_{hh}")
                        if hh == 0:
                            wsrc = A_wt[:, 0:4096]
                        else:
                            wsrc = (A_wt[:, 4096:8192] if oc % 2 == 0
                                    else A_kv[:, 0:4096])
                        for i in range(32):
                            hc = hh * 32 + i
                            hid = (A_q if hc < 32 else A_exp)[
                                :, (hc % 32) * 512:(hc % 32) * 512 + 512]
                            mm = tensor.matmul(
                                bank(4 + oc % 2),
                                wsrc[:, i * 128:i * 128 + 128],
                                hid,
                                start=(hc == 0), stop=(hc == HC - 1))
                        TK(mm, "tp", 1, f"downmm_{tg}_{oc}_{hh}")
                W(f"x3_{tg}_{EC - 1}")
                for ec in range(EC):
                    mm = tensor.matmul(bank(6)[0:1, :], onescol_f, xs(ec, tg * 512, 512),
                                       start=(ec == 0), stop=(ec == EC - 1))
                TK(mm, "tp", 1, f"meanmm_1_{tg}")
                for ec in range(EC):
                    W(f"sq_1_{tg}_{ec}")
                    TK(tensor.matmul(bank(7)[0:1, :], onescol_b,
                                     A_rp[:, (ec % 2) * 512:(ec % 2) * 512 + 512],
                                     start=(ec == 0), stop=(ec == EC - 1)),
                       "tp", 1, f"sqmm_1_{tg}_{ec}")
                W(f"rrow_1_{tg}", f"nmr_{tg}")
                for tsub in range(4):
                    TK(tensor.matmul(bank(6)[:, tsub:tsub + 1],
                                     rowC[0:1, tsub * 128:tsub * 128 + 128], ones1_f,
                                     start=True, stop=True), "tp", 1)
                    TK(tensor.matmul(bank(7)[:, tsub:tsub + 1],
                                     rowA[0:1, tsub * 128:tsub * 128 + 128], ones1_f,
                                     start=True, stop=True), "tp", 1, f"colmm_{tg}_{tsub}")
                for tsub in range(4):
                    grp = tsub % 2
                    W(f"ln2evac_{tg}_{tsub - 2}", f"cols_{tg}_3")
                    for ec in range(EC):
                        mm = tensor.transpose(
                            pAll[:, grp * 2048 + ec * 128:grp * 2048 + ec * 128 + 128],
                            xs(ec, tg * 512 + tsub * 128, 128), identsl)
                    TK(mm, "tp", 1, f"trans_{tg}_{tsub}")

        @block.gpsimd
        def _(gpsimd):
            W = lambda *e: C.wait(gpsimd, "gpsimd", *e)
            TK = lambda i, sem, n, ev=None: C.tick(i, sem, n, ev)
            TK(gpsimd.dma_start(out=warm_a, in_=qb[0:1, 0:4]), "qR", 16)
            C.wait_total(gpsimd, "gpsimd", "qR")
            TK(gpsimd.collective_compute("AllGather", ALU.bypass,
                                         replica_groups=ALL_GROUP,
                                         ins=[warm_a.opt()], outs=[warm_b.opt()]),
               "cc0", 1, "cc_warm")
            TK(gpsimd.dma_start(
                out=A_xb.ap().rearrange("p (ec t) -> p ec t", t=1024),
                in_=xT.rearrange("(ec p) t -> p ec t", p=128)), "qXB", 16, "xbload")
            TK(gpsimd.dma_start(out=A_exp[0:1, 0:2048], in_=qb), "qR", 16)
            TK(gpsimd.dma_start(out=A_exp[0:1, 2048:4096], in_=kb), "qR", 16)
            TK(gpsimd.dma_start(out=A_exp[0:1, 4096:6144], in_=vb), "qR", 16, "biasload")

            def slab_dma(dst_buf, src, c0, ev):
                TK(gpsimd.dma_start(
                    out=A_wt.ap()[:, dst_buf * 4096:dst_buf * 4096 + 4096]
                        .rearrange("p (ec c) -> p ec c", c=256),
                    in_=src[:, c0:c0 + 256].rearrange("(ec p) c -> p ec c", p=128)),
                   f"qW{dst_buf}", 16, ev)

            slab_src = [(kvwT, s * 256, f"kvw_{s}") for s in range(16)] + \
                       [(qwT, s * 256, f"qw_{s}") for s in range(8)]
            done_ev = [f"kvslab_done_{s}" for s in range(8)] + \
                      [f"vslab_done_{s}" for s in range(8, 16)] + \
                      [f"qslab_done_{s}" for s in range(8)]
            kst = [0]

            def kstores(upto):
                while kst[0] < min(upto, 32):
                    g = kst[0]
                    kst[0] += 1
                    s2, co = g // 4, 2 * (g // 4) + (g % 4) // 2
                    tg = g % 2
                    W(f"krope_{g}")
                    TK(gpsimd.dma_start(
                        out=k_send[co * 128:(co + 1) * 128, tg * 512:(tg + 1) * 512],
                        in_=A_kv[:, (g % 2) * 512:(g % 2) * 512 + 512]),
                       f"qKS{g % 2}", 16, f"kstore_{g}")

            def tile_in_g(i):
                r, jj = i // 16, i % 16
                if r < 2:
                    srcw = (upT_s, gateT_s)[r]
                    rr, cc2 = jj // 8, jj % 8
                    return srcw[rr * 128:(rr + 1) * 128, cc2 * 1024:(cc2 + 1) * 1024]
                return downT_s[jj]

            p1g = [0]

            def p1loads(n):
                for _ in range(n):
                    if _PHASE < 2 or p1g[0] >= 48:
                        return
                    i = p1g[0]
                    p1g[0] += 1
                    W(f"p1_{i - 8}")
                    TK(gpsimd.dma_start(
                        out=A_exp[:, 6656 + (i % 8) * 1024:6656 + (i % 8) * 1024 + 1024],
                        in_=tile_in_g(i)), f"qPA{i % 8}", 16, f"p1load_{i}")

            for j in range(16):
                if 2 <= j < 10:
                    kstores((j - 1) * 4)
                if j == 10:
                    kstores(32)
                    W("kstore_30", "kstore_31")
                    TK(gpsimd.collective_compute("AllGather", ALU.bypass,
                                                 replica_groups=KV_GROUPS,
                                                 ins=[k_send.opt()], outs=[k_full.opt()]),
                       "cc1", 1, "cc_k")
                if j >= 2:
                    W(done_ev[j - 2])
                src, c0, ev = slab_src[j]
                slab_dma(j % 2, src, c0, ev)
                p1loads(2)
            kstores(32)
            for j in range(16, 24):
                W(done_ev[j - 2])
                src, c0, ev = slab_src[j]
                slab_dma(j % 2, src, c0, ev)
                p1loads(1)
            W("vstore_62", "vstore_63")
            TK(gpsimd.collective_compute("AllGather", ALU.bypass,
                                         replica_groups=KV_GROUPS,
                                         ins=[v_send.opt()], outs=[v_full.opt()]),
               "cc2", 1, "cc_v")
            p1loads(48)
            if _PHASE >= 2:
                W("sclstore")
                TK(gpsimd.collective_compute("AllReduce", ALU.add,
                                             replica_groups=ALL_GROUP,
                                             ins=[scl_send.opt()], outs=[scl_full.opt()]),
                   "cc3", 1, "cc_scl")
            if _PHASE >= 2:
                W("qtstore_15")
                TK(gpsimd.collective_compute("AllGather", ALU.bypass,
                                             replica_groups=ALL_GROUP,
                                             ins=[up_send.opt()], outs=[up_full.opt()]),
                   "cc4", 1, "cc_up")
                W("qtstore_31")
                TK(gpsimd.collective_compute("AllGather", ALU.bypass,
                                             replica_groups=ALL_GROUP,
                                             ins=[gate_send.opt()], outs=[gate_full.opt()]),
                   "cc5", 1, "cc_gate")
                W("qtstore_47")
                TK(gpsimd.collective_compute("AllGather", ALU.bypass,
                                             replica_groups=ALL_GROUP,
                                             ins=[down_send.opt()], outs=[down_full.opt()]),
                   "cc6", 1, "cc_down")
            if _PHASE >= 3.3:
                for qg in range(2 if _PHASE >= 3.9 else 1):
                    for s in range(8):
                        W(f"oproj_slabdone_{qg}_{s - 2}" if s >= 2 else
                          (f"oproj_slabdone_{qg - 1}_6" if qg == 1 else "qslab_done_7"))
                        if qg == 1 and s == 1:
                            W("oproj_slabdone_0_7")
                        slab_dma(s % 2, owT, s * 256, f"owload_{qg}_{s}")
            if _PHASE == 3.06:
                W("dbg2")
                TK(gpsimd.dma_start(out=out_s[0:128, 0:1024], in_=A_sc[:, 0:1024]),
                   "qOS", 16, "dbgstore")
            if _PHASE in (3.95, 4.5):
                W("oadd_1_15" if _PHASE == 3.95 else "x2done")
                TK(gpsimd.dma_start(out=out_s[0:128, :], in_=A_x[:, 0:2048]),
                   "qOS", 16, "dbgx")
            if _PHASE == 2.5:
                W("cc_up")
                TK(gpsimd.dma_start(out=out_s[0:128, :],
                                    in_=up_full[0, 0, :, 0:2048]), "qOS", 16, "dbgu")
            for tg in range(2 if (_PHASE >= 5 and _PHASE != 5.5) else 0):
                W("cc_down")
                for oc in range(16):
                    if oc == 0:
                        W(f"gatemm_{tg}_31")
                    elif oc == 1:
                        W("avdone_1_1" if tg == 0 else "downmm_0_15_1")
                    else:
                        W(f"downmm_{tg}_{oc - 2}_1")
                    dst = (A_wt.ap()[:, 4096:8192] if oc % 2 == 0
                           else A_kv.ap()[:, 0:4096])
                    TK(gpsimd.dma_start(
                        out=dst.rearrange("p (r f) -> p r f", f=1024),
                        in_=down_full[4:8, oc, :, :]
                            .rearrange("r p f -> p r f")),
                       "qD1", 16, f"dload_{tg}_{oc}_1")
                for tsub in range(4):
                    W(f"ln2evac_{tg}_{tsub}")
                    TK(gpsimd.dma_start(
                        out=out_s[tg * 512 + tsub * 128:tg * 512 + tsub * 128 + 128, :],
                        in_=A_sc[:, 0:2048]), "qOS", 16, f"ostore_{tg}_{tsub}")
            if _PHASE == 5.5:
                W("x3_1_15")
                TK(gpsimd.dma_start(out=out_s[0:128, :], in_=A_x[:, 0:2048]),
                   "qOS", 16, "dbgx3")
            if _PHASE == 5.6:
                W("x3_1_15")
                TK(gpsimd.dma_start(out=out_dbg, in_=A_x[:, :]), "qOS", 16, "dbgx3f")
            if _PHASE == 5.7:
                W("x3_1_15")
                TK(gpsimd.dma_start(out=out_dbg[:, 0:2048], in_=s_mf[:, :]),
                   "qOS", 16, "dbg57a")
                TK(gpsimd.dma_start(out=out_dbg[:, 2048:4096], in_=A_q[:, 0:2048]),
                   "qOS", 16, "dbg57b")
                TK(gpsimd.dma_start(out=out_dbg[:, 4096:6144], in_=A_exp[:, 0:2048]),
                   "qOS", 16, "dbg57c")
                TK(gpsimd.dma_start(out=out_dbg[:, 6144:8192], in_=A_xb[:, 0:2048]),
                   "qOS", 16, "dbg57d")
            C.wait_total(gpsimd, "gpsimd", "qOS")

    return nc, C.events


_CACHE = {}


def _get_nc():
    if "nc" not in _CACHE:
        _, events = _build(None)
        nc, events2 = _build(events)
        _CACHE["nc"] = nc
    return _CACHE["nc"]


def _rope_tables():
    half = 64
    inv = 1.0 / (10000.0 ** (np.arange(half, dtype=np.float64) / half))
    pos = np.arange(S, dtype=np.float64)
    ang = pos[None, :] * inv[:, None]
    ang = np.concatenate([ang, ang], axis=0)
    cos, sin = np.cos(ang), np.sin(ang)
    out = []
    for t64 in (cos, sin):
        hi = t64.astype(np.float32)
        lo = (t64 - hi.astype(np.float64)).astype(np.float32)
        out.append((hi, lo))
    return out


def kernel(L0_attn_x, L0_attn_q_w, L0_attn_q_b, L0_attn_kv_w, L0_attn_kv_b,
           L0_attn_o_w, L0_ffn_W_up, L0_ffn_W_gate, L0_ffn_W_down):
    f32 = np.float32
    x = np.asarray(L0_attn_x, f32)
    qwT = np.ascontiguousarray(np.asarray(L0_attn_q_w, f32).T)
    kvwT = np.ascontiguousarray(np.asarray(L0_attn_kv_w, f32).T)
    owT = np.ascontiguousarray(np.asarray(L0_attn_o_w, f32).T)
    qb = np.ascontiguousarray(np.asarray(L0_attn_q_b, f32).reshape(1, E))
    kvb = np.asarray(L0_attn_kv_b, f32)
    kb = np.ascontiguousarray(kvb[:E].reshape(1, E))
    vb = np.ascontiguousarray(kvb[E:].reshape(1, E))
    upT = np.ascontiguousarray(np.asarray(L0_ffn_W_up, f32).T)
    gateT = np.ascontiguousarray(np.asarray(L0_ffn_W_gate, f32).T)
    downT = np.ascontiguousarray(np.asarray(L0_ffn_W_down, f32).T)

    (cos_hi, cos_lo), (sin_hi, sin_lo) = _rope_tables()
    R = np.zeros((128, 128), f32)
    for d in range(64):
        R[d + 64, d] = -1.0
        R[d, d + 64] = 1.0
    ident = np.eye(128, dtype=f32)

    nc = _get_nc()
    in_maps = []
    for c in range(NCORE):
        b, q4 = c // 4, c % 4
        t0 = q4 * T
        xT_c = np.ascontiguousarray(x[b, t0:t0 + T, :].T)
        down_sl = downT[c * 1024:(c + 1) * 1024, :]
        down_tiled = np.ascontiguousarray(
            down_sl.reshape(8, 128, 16, 128).transpose(2, 1, 0, 3).reshape(16, 128, 1024))
        in_maps.append(dict(
            xT=xT_c, qwT=qwT, kvwT=kvwT, owT=owT, qb=qb, kb=kb, vb=vb,
            upT_s=np.ascontiguousarray(upT[c * 256:(c + 1) * 256, :]),
            gateT_s=np.ascontiguousarray(gateT[c * 256:(c + 1) * 256, :]),
            downT_s=down_tiled,
            cos_hi=np.ascontiguousarray(cos_hi[:, t0:t0 + T]),
            cos_lo=np.ascontiguousarray(cos_lo[:, t0:t0 + T]),
            sin_hi=np.ascontiguousarray(sin_hi[:, t0:t0 + T]),
            sin_lo=np.ascontiguousarray(sin_lo[:, t0:t0 + T]),
            rmat=R, ident=ident))

    res = run_bass_kernel_spmd(nc, in_maps, core_ids=list(range(NCORE)))
    out = np.empty((B, S, E), f32)
    for c in range(NCORE):
        out[c // 4, (c % 4) * T:(c % 4 + 1) * T, :] = res.results[c]["out"]
    return out


# version 25 (final)
# speedup vs baseline: 1.0003x; 1.0003x over previous
import numpy as np
from contextlib import ExitStack

import concourse.bass as bass
import concourse.mybir as mybir
from concourse.bass_utils import run_bass_kernel_spmd

F32 = mybir.dt.float32
BF16 = mybir.dt.bfloat16
AF = mybir.ActivationFunctionType
ALU = mybir.AluOpType

B, S, E, H = 2, 4096, 2048, 8192
T = 1024
EC = 16
TC = 32
HC = 64
NCORE = 8
HS = float(128) ** -0.5
CEXP = 12.0
NW = float(E * H)
EPS_T = 1e-8
EPS_LN = 1e-5

KV_GROUPS = [[0, 1, 2, 3], [4, 5, 6, 7]]
ALL_GROUP = [list(range(8))]

SEMS = ["qX", "qR", "qB", "qXB", "qK0", "qK1", "qV0", "qV1", "qU", "qGt",
        "qD0", "qD1", "qW0", "qW1", "qKS0", "qKS1", "qVS0", "qVS1",
        "qP0", "qP1", "qPA0", "qPA1", "qPA2", "qPA3", "qPA4", "qPA5", "qPA6", "qPA7", "qQSu", "qQSg", "qQSd", "qOS",
        "tp", "vp", "ap",
        "cc0", "cc1", "cc2", "cc3", "cc4", "cc5", "cc6"]


class _Ctx:

    def __init__(self, prev, handles):
        self.real = prev is not None
        self.prev = prev or {}
        self.h = handles
        self.counts = {s: 0 for s in SEMS}
        self.events = {}
        self.waited = {}

    def tick(self, instr, sem, n, ev=None):
        self.counts[sem] += n
        if self.real:
            instr.then_inc(self.h[sem], n)
        if ev is not None:
            assert ev not in self.events, f"duplicate event {ev}"
            self.events[ev] = (sem, self.counts[sem])
            if self.real:
                assert self.prev[ev] == self.events[ev], (
                    f"pass drift at {ev}: {self.prev[ev]} vs {self.events[ev]}")
        return instr

    def ev(self, ev, sem):
        assert ev not in self.events, f"duplicate event {ev}"
        self.events[ev] = (sem, self.counts[sem])
        if self.real:
            assert self.prev[ev] == self.events[ev]

    def wait(self, eng, engname, *evs):
        if not self.real:
            return
        for evname in evs:
            if evname is None:
                continue
            if evname not in self.prev:
                assert "-" in evname or _PHASE < 99, f"unknown event {evname}"
                continue
            sem, val = self.prev[evname]
            if val <= 0:
                continue
            key = (engname, sem)
            if self.waited.get(key, -1) >= val:
                continue
            self.waited[key] = val
            eng.wait_ge(self.h[sem], val)

    def wait_total(self, eng, engname, sem):
        val = self.counts[sem]
        if not self.real or val <= 0:
            return
        key = (engname, sem)
        if self.waited.get(key, -1) >= val:
            return
        self.waited[key] = val
        eng.wait_ge(self.h[sem], val)


_PHASE = 99


def _build(prev=None):
    nc = bass.Bass("TRN2", target_bir_lowering=False,
                   detect_race_conditions=False)
    dt_in = {}

    def din(name, shape):
        dt_in[name] = nc.dram_tensor(name, shape, F32, kind="ExternalInput").ap()
        return dt_in[name]

    xT = din("xT", [E, T])
    qwT = din("qwT", [E, E])
    kvwT = din("kvwT", [E, 2 * E])
    owT = din("owT", [E, E])
    qb = din("qb", [1, E])
    kb = din("kb", [1, E])
    vb = din("vb", [1, E])
    upT_s = din("upT_s", [256, H])
    gateT_s = din("gateT_s", [256, H])
    downT_s = din("downT_s", [16, 128, 1024])
    cos_hi = din("cos_hi", [128, T])
    cos_lo = din("cos_lo", [128, T])
    sin_hi = din("sin_hi", [128, T])
    sin_lo = din("sin_lo", [128, T])
    rmat = din("rmat", [128, 128])
    ident = din("ident", [128, 128])

    out_s = nc.dram_tensor("out", [T, E], F32, kind="ExternalOutput").ap()
    out_dbg = nc.dram_tensor("out_dbg", [128, 16384], F32, kind="ExternalOutput").ap()

    k_send = nc.dram_tensor("k_send", [E, T], BF16).ap()
    v_send = nc.dram_tensor("v_send", [T, E], BF16).ap()
    k_full = nc.dram_tensor("k_full", [4, EC, 128, T], BF16).ap()
    v_full = nc.dram_tensor("v_full", [16, 2, 128, E], BF16).ap()
    up_send = nc.dram_tensor("up_send", [256, H], BF16).ap()
    gate_send = nc.dram_tensor("gate_send", [256, H], BF16).ap()
    down_send = nc.dram_tensor("down_send", [16, 128, 1024], BF16).ap()
    up_full = nc.dram_tensor("up_full", [8, 2, 128, H], BF16).ap()
    gate_full = nc.dram_tensor("gate_full", [8, 2, 128, H], BF16).ap()
    down_full = nc.dram_tensor("down_full", [8, 16, 128, 1024], BF16).ap()
    scl_send = nc.dram_tensor("scl_send", [1, 4], F32).ap()
    scl_full = nc.dram_tensor("scl_full", [1, 4], F32).ap()
    warm_a = nc.dram_tensor("warm_a", [1, 4], F32).ap()
    warm_b = nc.dram_tensor("warm_b", [8, 4], F32).ap()

    with ExitStack() as ctx:
        sb = lambda n, s, dt: ctx.enter_context(nc.sbuf_tensor(n, s, dt))
        A_x = sb("A_x", [128, 16384], F32)
        A_xb = sb("A_xb", [128, 16384], BF16)
        A_q = sb("A_q", [128, 16384], BF16)
        A_exp = sb("A_exp", [128, 16384], BF16)
        A_kv = sb("A_kv", [128, 4096], BF16)
        A_wt = sb("A_wt", [128, 8192], BF16)
        A_rp = sb("A_rp", [128, 2048], BF16)
        A_sc = sb("A_sc", [128, 2048], F32)
        s_qt = sb("s_qt", [128, 1024], BF16)
        s_sm = sb("s_sm", [128, 648], BF16)
        s_mf = sb("s_mf", [128, 2048], F32)
        s_acc = sb("s_acc", [128, 52], F32)
        pAll = ctx.enter_context(nc.psum_tensor("pAll", [128, 4096], F32))

        handles = {s: ctx.enter_context(nc.semaphore(s)) for s in SEMS} \
            if prev is not None else {s: None for s in SEMS}
        C = _Ctx(prev, handles)
        block = ctx.enter_context(nc.Block())

        bank = lambda i, w=512: pAll[:, i * 512:i * 512 + w]

        def xs(ec, lo, n):
            return A_x[:, ec * 1024 + lo: ec * 1024 + lo + n]

        def xbs(ec, lo, n):
            return A_xb[:, ec * 1024 + lo: ec * 1024 + lo + n]

        def qslice(ec, lo, n):
            return A_q[:, ec * 1024 + lo: ec * 1024 + lo + n]

        ones512_b = s_sm[0:1, 0:512]
        ones128r_b = s_sm[0:1, 512:640]
        onescol_b = s_sm[:, 640:641]
        onescol_f = s_mf[:, 0:1]
        ones1_f = s_mf[0:1, 1:2]
        ones128r_f = s_mf[0:1, 2:130]
        scol = lambda j: s_mf[:, 130 + j:131 + j]
        srow = s_mf[0:1, 162:166]
        sclback_row = s_mf[0:1, 166:170]
        drow = s_mf[0:1, 150:158]
        rowA = s_mf[0:1, 178:690]
        recB = s_mf[:, 690:1202]
        rowB = s_mf[0:1, 690:1202]
        identsl = s_mf[:, 1202:1330]
        rowC = s_mf[0:1, 1330:1842]
        ln2col = lambda j: s_mf[:, 1842 + j:1843 + j]
        rmatsl = s_mf[:, 1858:1986]
        c_negC = s_mf[:, 1986:1987]
        c_eps = s_mf[:, 1987:1988]

        qb_b = lambda lo, n: A_exp[0:1, lo:lo + n]
        kb_b = lambda lo, n: A_exp[0:1, 2048 + lo:2048 + lo + n]
        vb_b = lambda lo, n: A_exp[0:1, 4096 + lo:4096 + lo + n]

        cos_sl = lambda tg: A_rp[:, tg * 512:(tg + 1) * 512]
        sin_sl = lambda tg: A_rp[:, 1024 + tg * 512:1024 + tg * 512 + 512]

        def kq_groups(nslab):
            for s in range(nslab):
                for ci in range(2):
                    for tg in range(2):
                        yield s, 2 * s + ci, tg, s * 4 + ci * 2 + tg

        @block.sync
        def _(sync):
            W = lambda *e: C.wait(sync, "sync", *e)
            TK = lambda i, sem, n, ev=None: C.tick(i, sem, n, ev)

            TK(sync.dma_start(out=identsl, in_=ident), "qX", 16)
            TK(sync.dma_start(out=rmatsl, in_=rmat), "qX", 16)
            TK(sync.dma_start(
                out=A_x.ap().rearrange("p (ec t) -> p ec t", t=1024),
                in_=xT.rearrange("(ec p) t -> p ec t", p=128)), "qX", 16, "xload")
            TK(sync.dma_start(out=A_sc[:, 0:1024], in_=cos_hi), "qX", 16)
            TK(sync.dma_start(out=A_sc[:, 1024:2048], in_=cos_lo), "qX", 16, "coshl")
            W("cosdone")
            TK(sync.dma_start(out=A_sc[:, 0:1024], in_=sin_hi), "qX", 16)
            TK(sync.dma_start(out=A_sc[:, 1024:2048], in_=sin_lo), "qX", 16, "sinhl")
            C.ev("p0done", "qX")

            if _PHASE < 3:
                return
            for qg in range(2 if _PHASE >= 3.9 else 1):
                W("cc_k")
                W("qlastmm")
                for ks in range(16):
                    a = qg * 16 + ks - 2
                    W(f"scoreslab_{a // 16}_{a % 16}" if a >= 0 else None)
                    rank, tl0 = ks // 4, (ks % 4) * 256
                    TK(sync.dma_start(
                        out=A_xb.ap()[:, (ks % 2) * 4096:(ks % 2) * 4096 + 4096]
                            .rearrange("p (ec t) -> p ec t", t=256),
                        in_=k_full[rank][:, :, tl0:tl0 + 256]
                            .transpose([1, 0, 2])),
                       f"qK{ks % 2}", 16, f"kload_{qg}_{ks}")
                if _PHASE < 3.2:
                    continue
                W("cc_v")
                for eh in range(2):
                    for vs in range(16):
                        a = (qg * 2 + eh) * 16 + vs - 2
                        W(f"avslab_{a // 32}_{(a % 32) // 16}_{a % 16}" if a >= 0 else None)
                        TK(sync.dma_start(
                            out=A_kv.ap()[:, (vs % 2) * 2048:(vs % 2) * 2048 + 2048]
                                .rearrange("p (c e) -> p c e", e=1024),
                            in_=v_full[vs][:, :, eh * 1024:(eh + 1) * 1024]
                                .transpose([1, 0, 2])),
                           f"qV{vs % 2}", 16, f"vload_{qg}_{eh}_{vs}")

            if _PHASE < 5:
                return
            for tg in range(2):
                W("cc_up")
                W("cc_gate")
                if tg == 0:
                    W("oproj_slabdone_1_6", "oproj_slabdone_1_7")
                if tg == 1:
                    W("downmm_0_15_1")
                for j in range(32):
                    W(f"upmm_{tg}_{j - 1}")
                    for rl in range(2):
                        TK(sync.dma_start(
                            out=A_wt.ap()[:, 0:4096]
                                .rearrange("p (r rl c) -> p r rl c", rl=2, c=256)[:, :, rl, :],
                            in_=up_full[:, rl, :, j * 256:(j + 1) * 256]
                                .rearrange("r p c -> p r c")),
                           "qU", 16, f"uload_{tg}_{j}" if rl == 1 else None)
                    W(f"gatemm_{tg}_{j - 1}")
                    for rl in range(2):
                        TK(sync.dma_start(
                            out=A_wt.ap()[:, 4096:8192]
                                .rearrange("p (r rl c) -> p r rl c", rl=2, c=256)[:, :, rl, :],
                            in_=gate_full[:, rl, :, j * 256:(j + 1) * 256]
                                .rearrange("r p c -> p r c")),
                           "qGt", 16, f"gload_{tg}_{j}" if rl == 1 else None)
                W("cc_down")
                for oc in range(16):
                    for hh in range(2):
                        d = oc * 2 + hh
                        if d == 0:
                            W(f"upmm_{tg}_31")
                        elif d == 1:
                            W(f"gatemm_{tg}_31")
                        else:
                            W(f"downmm_{tg}_{(d - 2) // 2}_{(d - 2) % 2}")
                        TK(sync.dma_start(
                            out=A_wt.ap()[:, (d % 2) * 4096:(d % 2) * 4096 + 4096]
                                .rearrange("p (r f) -> p r f", f=1024),
                            in_=down_full[hh * 4:(hh + 1) * 4, oc, :, :]
                                .rearrange("r p f -> p r f")),
                           f"qD{(oc * 2 + hh) % 2}", 16, f"dload_{tg}_{oc}_{hh}")

        @block.scalar
        def _(scalar):
            W = lambda *e: C.wait(scalar, "scalar", *e)
            TK = lambda i, sem, n, ev=None: C.tick(i, sem, n, ev)
            for s, co, tg, g in kq_groups(8):
                W(f"kproj_{g}", f"krope_{g - 2}")
                TK(scalar.activation(out=A_sc[:, (g % 2) * 512:(g % 2) * 512 + 512],
                                     in_=bank(g % 2), func=AF.Copy),
                   "ap", 1, f"kevac_{g}")
            for vg in range(64):
                W(f"vevac_{vg}")
                s, tc = vg // 8, vg % 8
                TK(scalar.dma_start(
                    out=v_send[tc * 128:(tc + 1) * 128, s * 256:(s + 1) * 256],
                    in_=A_kv[:, 1024 + (vg % 2) * 256:1024 + (vg % 2) * 256 + 256]),
                   f"qVS{vg % 2}", 16, f"vstore_{vg}")
            for s, co, tg, g in kq_groups(8):
                W(f"qproj_{g}", f"qrope_{g - 2}")
                TK(scalar.activation(out=A_sc[:, (g % 2) * 512:(g % 2) * 512 + 512],
                                     in_=bank(g % 2), func=AF.Copy, scale=HS),
                   "ap", 1, f"qevac_{g}")
            if _PHASE < 2:
                return
            def tile_in(i):
                r, j = i // 16, i % 16
                src = (upT_s, gateT_s)[r] if r < 2 else None
                if r < 2:
                    rr, cc2 = j // 8, j % 8
                    return src[rr * 128:(rr + 1) * 128, cc2 * 1024:(cc2 + 1) * 1024]
                return downT_s[j]

            def tile_out(i):
                r, j = i // 16, i % 16
                dst = (up_send, gate_send)[r] if r < 2 else None
                if r < 2:
                    rr, cc2 = j // 8, j % 8
                    return dst[rr * 128:(rr + 1) * 128, cc2 * 1024:(cc2 + 1) * 1024]
                return down_send[j]

            W("qdone")
            W("sclrow")
            TK(scalar.dma_start(out=scl_send, in_=srow), "qB", 16, "sclstore")
            W("cc_scl")
            TK(scalar.dma_start(out=sclback_row, in_=scl_full), "qB", 16, "sclload")
            qacts = []
            for _i in range(48):
                qacts.append(("L", _i))
                if _i >= 2:
                    qacts.append(("S", _i - 2))
            qacts += [("S", 46), ("S", 47)]
            qpos = [0]

            def quant_dmas(n):
                for _ in range(n):
                    if qpos[0] >= len(qacts):
                        return
                    kind, i = qacts[qpos[0]]
                    qpos[0] += 1
                    if kind == "L":
                        W(f"quant_{i - 2}")
                        TK(scalar.dma_start(
                            out=A_sc[:, (i % 2) * 1024:(i % 2) * 1024 + 1024],
                            in_=tile_in(i)), f"qP{i % 2}", 16, f"qtload_{i}")
                    else:
                        W(f"quant_{i}")
                        TK(scalar.dma_start(out=tile_out(i), in_=s_qt[:, 0:1024]),
                           ("qQSu", "qQSg", "qQSd")[i // 16], 16, f"qtstore_{i}")

            quant_dmas(len(qacts))

            if _PHASE < 3:
                return
            if _PHASE == 3.06:
                W("score_0_0")
                TK(scalar.activation(out=A_sc[:, 0:512], in_=bank(0), func=AF.Copy),
                   "ap", 1, "dbg1")
                W("score_0_1")
                TK(scalar.activation(out=A_sc[:, 512:1024], in_=bank(1), func=AF.Copy),
                   "ap", 1, "dbg2")
            for qg in range(2 if _PHASE >= 3.9 else 1):
                for tc in range((4 if _PHASE == 3.066 else TC) if _PHASE >= 3.065 else 0):
                    W(f"score_{qg}_{tc}")
                    if qg == 1 and tc == 0:
                        W("avdone_0_1")
                    if _PHASE == 3.065:
                        TK(scalar.activation(out=A_exp[:, tc * 512:(tc + 1) * 512],
                                             in_=bank(tc % 2), func=AF.Copy),
                           "ap", 1, f"exp_{qg}_{tc}")
                    else:
                        TK(scalar.activation(out=A_exp[:, tc * 512:(tc + 1) * 512],
                                             in_=bank(tc % 2), func=AF.Exp,
                                             bias=c_negC),
                           "ap", 1, f"exp_{qg}_{tc}")
                if _PHASE >= 3.09:
                    W(f"summm_{qg}_{TC - 1}")
                    TK(scalar.activation(out=rowA, in_=bank(2)[0:1, :], func=AF.Copy),
                       "ap", 1, f"sumevac_{qg}")
            if _PHASE < 4:
                return
            W("oadd_1_15")
            for tg in range(2):
                for ec in range(EC):
                    W(f"sqmm_0_{tg}_{ec - 2}")
                    TK(scalar.activation(out=A_rp[:, (ec % 2) * 512:(ec % 2) * 512 + 512],
                                         in_=xs(ec, tg * 512, 512), func=AF.Square),
                       "ap", 1, f"sq_0_{tg}_{ec}")
                W(f"meanmm_0_{tg}")
                TK(scalar.activation(out=rowA, in_=bank(2)[0:1, :], func=AF.Copy,
                                     scale=1.0 / E), "ap", 1, f"mrow_0_{tg}")
                W(f"sqmm_0_{tg}_{EC - 1}")
                TK(scalar.activation(out=rowB, in_=bank(3)[0:1, :], func=AF.Copy,
                                     scale=1.0 / E), "ap", 1, f"sqrow_0_{tg}")
                W(f"var_0_{tg}")
                TK(scalar.activation(out=rowC, in_=rowB, func=AF.Sqrt,
                                     bias=c_eps[0:1, :]), "ap", 1, f"sqrt_0_{tg}")

            if _PHASE < 5:
                return
            for tg in range(2):
                W("x2done", f"ostore_{tg - 1}_3")
                for hc in range(HC):
                    W(f"upg_{tg}_{hc}", f"hid_{tg}_{hc - 2}")
                    TK(scalar.activation(out=A_sc[:, (hc % 2) * 512:(hc % 2) * 512 + 512],
                                         in_=bank(hc % 2), func=AF.Silu,
                                         scale=scol(0)), "ap", 1, f"silu_{tg}_{hc}")
                for ec in range(EC):
                    W(f"x3_{tg}_{ec}")
                    W(f"sqmm_1_{tg}_{ec - 2}")
                    TK(scalar.activation(out=A_rp[:, (ec % 2) * 512:(ec % 2) * 512 + 512],
                                         in_=xs(ec, tg * 512, 512), func=AF.Square),
                       "ap", 1, f"sq_1_{tg}_{ec}")
                W(f"meanmm_1_{tg}")
                TK(scalar.activation(out=rowA, in_=bank(6)[0:1, :], func=AF.Copy,
                                     scale=1.0 / E), "ap", 1, f"mrow_1_{tg}")
                W(f"sqmm_1_{tg}_{EC - 1}")
                TK(scalar.activation(out=rowB, in_=bank(7)[0:1, :], func=AF.Copy,
                                     scale=1.0 / E), "ap", 1, f"sqrow_1_{tg}")
                W(f"var_1_{tg}")
                TK(scalar.activation(out=rowC, in_=rowB, func=AF.Sqrt,
                                     bias=c_eps[0:1, :]), "ap", 1, f"sqrt_1_{tg}")

        @block.vector
        def _(vector):
            W = lambda *e: C.wait(vector, "vector", *e)
            TK = lambda i, sem, n, ev=None: C.tick(i, sem, n, ev)
            W("coshl")
            TK(vector.tensor_add(A_rp[:, 0:1024], A_sc[:, 0:1024], A_sc[:, 1024:2048]),
               "vp", 1, "cosdone")
            W("sinhl")
            TK(vector.tensor_add(A_rp[:, 1024:2048], A_sc[:, 0:1024], A_sc[:, 1024:2048]),
               "vp", 1, "sindone")
            vector.memset(ones512_b, 1.0)
            vector.memset(ones128r_b, 1.0)
            vector.memset(onescol_b, 1.0)
            vector.memset(onescol_f, 1.0)
            vector.memset(ones1_f, 1.0)
            vector.memset(srow, 0.0)
            vector.memset(c_negC, -CEXP)
            vector.memset(c_eps, EPS_LN)
            TK(vector.memset(ones128r_f, 1.0), "vp", 1, "ones")
            p1i = [0]

            def p1slot():
                if _PHASE < 2 or p1i[0] >= 48:
                    return
                i = p1i[0]
                p1i[0] += 1
                W(f"p1load_{i}")
                TK(vector.tensor_reduce(
                    out=s_acc[:, i:i + 1],
                    in_=A_exp[:, 6656 + (i % 8) * 1024:6656 + (i % 8) * 1024 + 1024],
                    axis=mybir.AxisListType.X, op=ALU.add,
                    apply_absolute_value=True), "vp", 1, f"p1_{i}")

            def p1drain(n):
                for _ in range(n):
                    if p1i[0] >= 48:
                        return
                    p1slot()

            for s, co, tg, g in kq_groups(8):
                W(f"rotmm_K_{g}")
                TK(vector.scalar_tensor_tensor(
                    out=A_sc[:, 1024 + (g % 2) * 512:1024 + (g % 2) * 512 + 512],
                    in0=bank(2 + g % 2), scalar=1.0, in1=sin_sl(tg),
                    op0=ALU.mult, op1=ALU.mult), "vp", 1, f"kt2_{g}")
                ktmp = A_sc[:, (g % 2) * 512:(g % 2) * 512 + 512]
                vector.tensor_mul(ktmp, ktmp, cos_sl(tg))
                W(f"kstore_{g - 2}")
                TK(vector.tensor_add(
                    A_kv[:, (g % 2) * 512:(g % 2) * 512 + 512], ktmp,
                    A_sc[:, 1024 + (g % 2) * 512:1024 + (g % 2) * 512 + 512]),
                   "vp", 1, f"krope_{g}")
                if g % 2 == 1:
                    p1slot()
            for vg in range(64):
                W(f"vproj_{vg}", f"vstore_{vg - 2}")
                TK(vector.tensor_copy(
                    out=A_kv[:, 1024 + (vg % 2) * 256:1024 + (vg % 2) * 256 + 256],
                    in_=bank(vg % 2, 256)), "vp", 1, f"vevac_{vg}")
                if vg % 4 == 0:
                    p1slot()
            for s, co, tg, g in kq_groups(8):
                W(f"rotmm_Q_{g}")
                TK(vector.scalar_tensor_tensor(
                    out=A_sc[:, 1024 + (g % 2) * 512:1024 + (g % 2) * 512 + 512],
                    in0=bank(2 + g % 2), scalar=1.0, in1=sin_sl(tg),
                    op0=ALU.mult, op1=ALU.mult), "vp", 1, f"qt2_{g}")
                qtmp = A_sc[:, (g % 2) * 512:(g % 2) * 512 + 512]
                vector.tensor_mul(qtmp, qtmp, cos_sl(tg))
                TK(vector.tensor_add(
                    qslice(co, tg * 512, 512), qtmp,
                    A_sc[:, 1024 + (g % 2) * 512:1024 + (g % 2) * 512 + 512]),
                   "vp", 1, f"qrope_{g}")
                if g % 4 == 3:
                    p1slot()
                if g == 31:
                    C.ev("qdone", "vp")
            if _PHASE < 2:
                return
            p1drain(48)
            for r in range(3):
                TK(vector.reduce_sum(out=s_acc[:, 48 + r:49 + r],
                                     in_=s_acc[:, r * 16:(r + 1) * 16],
                                     axis=mybir.AxisListType.X),
                   "vp", 1, "p1red" if r == 2 else None)
            W("sclsum")
            TK(vector.tensor_copy(out=srow[0:1, 0:3], in_=bank(3)[0:1, 0:3]),
               "vp", 1, "sclrow")
            W("thrbcast")
            TK(vector.tensor_scalar(out=s_mf[:, 132:135], in0=bank(3)[:, 0:3],
                                    scalar1=0.5 / NW, scalar2=0.5 * EPS_T,
                                    op0=ALU.mult, op1=ALU.add), "vp", 1)
            TK(vector.tensor_scalar(out=s_mf[:, 135:138], in0=bank(3)[:, 0:3],
                                    scalar1=-0.5 / NW, scalar2=-0.5 * EPS_T,
                                    op0=ALU.mult, op1=ALU.add), "vp", 1)
            TK(vector.tensor_scalar(out=s_mf[:, 130:131], in0=bank(3)[:, 0:1],
                                    scalar1=-1.0 / NW, scalar2=-EPS_T,
                                    op0=ALU.mult, op1=ALU.add), "vp", 1)
            TK(vector.tensor_scalar(out=s_mf[:, 131:132], in0=bank(3)[:, 1:2],
                                    scalar1=1.0 / NW, scalar2=EPS_T,
                                    op0=ALU.mult, op1=ALU.add), "vp", 1)
            TK(vector.tensor_scalar(out=s_mf[:, 138:139], in0=bank(3)[:, 2:3],
                                    scalar1=1.0 / NW, scalar2=EPS_T,
                                    op0=ALU.mult, op1=ALU.add),
               "vp", 1, "scols")
            vqi = [0]

            def vquant(n):
                for _ in range(n):
                    if vqi[0] >= 48:
                        return
                    i = vqi[0]
                    vqi[0] += 1
                    r = i // 16
                    wtile = A_sc[:, (i % 2) * 1024:(i % 2) * 1024 + 1024]
                    W(f"qtload_{i}", f"qtstore_{i - 1}")
                    TK(vector.tensor_single_scalar(out=s_qt[:, 0:1024], in_=wtile,
                                                   scalar=scol(2 + r), op=ALU.is_gt),
                       "vp", 1)
                    TK(vector.scalar_tensor_tensor(out=s_qt[:, 0:1024], in0=wtile,
                                                   scalar=scol(5 + r), in1=s_qt[:, 0:1024],
                                                   op0=ALU.is_lt, op1=ALU.subtract),
                       "vp", 1, f"quant_{i}")

            vquant(48)
            if _PHASE < 3:
                return
            for qg in range(2 if _PHASE >= 3.9 else 1):
                if _PHASE < 3.2:
                    continue
                W(f"sumevac_{qg}")
                TK(vector.reciprocal(out=rowA, in_=rowA), "vp", 1, f"recip_{qg}")
                W(f"recipbcmm_{qg}")
                TK(vector.tensor_copy(out=recB, in_=bank(0)), "vp", 1, f"recB_{qg}")
                for eh in range(2):
                    W(f"avdone_{qg}_{eh}")
                    if qg == 1:
                        W("oproj_0_15")
                    for e8 in range(8):
                        TK(vector.scalar_tensor_tensor(
                            out=A_xb[:, 8192 + (eh * 8 + e8) * 512:
                                     8192 + (eh * 8 + e8) * 512 + 512],
                            in0=bank(e8), scalar=1.0, in1=recB,
                            op0=ALU.mult, op1=ALU.mult),
                           "vp", 1, f"avevac_{qg}_{eh}_{e8}")
                for oc in range(EC if _PHASE >= 3.3 else 0):
                    W(f"oproj_{qg}_{oc}", "p0done")
                    TK(vector.tensor_add(xs(oc, qg * 512, 512), bank(oc % 2),
                                         xs(oc, qg * 512, 512)),
                       "vp", 1, f"oadd_{qg}_{oc}")
            if _PHASE < 4:
                return
            for tg in range(2):
                W(f"mrow_0_{tg}", f"sqrow_0_{tg}")
                TK(vector.tensor_mul(rowC, rowA, rowA), "vp", 1)
                TK(vector.tensor_sub(rowB, rowB, rowC), "vp", 1, f"var_0_{tg}")
                W(f"sqrt_0_{tg}")
                TK(vector.reciprocal(out=rowC, in_=rowC), "vp", 1, f"rrow_0_{tg}")
                W(f"bcastm_0_{tg}", "quant_47")
                TK(vector.tensor_copy(out=A_sc[:, tg * 512:tg * 512 + 512], in_=bank(2)),
                   "vp", 1, f"mb_{tg}")
                W(f"bcastr_0_{tg}")
                TK(vector.tensor_copy(out=A_sc[:, 1024 + tg * 512:1024 + tg * 512 + 512],
                                      in_=bank(3)), "vp", 1, f"rb_{tg}")
            for ec in range(EC):
                x_ec = xs(ec, 0, 1024)
                TK(vector.tensor_sub(x_ec, x_ec, A_sc[:, 0:1024]), "vp", 1)
                TK(vector.tensor_mul(x_ec, x_ec, A_sc[:, 1024:2048]), "vp", 1)
                TK(vector.tensor_copy(out=xbs(ec, 0, 1024), in_=x_ec),
                   "vp", 1, "x2done" if ec == EC - 1 else f"ln1_{ec}")
            if _PHASE < 5:
                return
            for tg in range(2):
                for hc in range(HC):
                    hid = (A_q if hc < 32 else A_exp)[:, (hc % 32) * 512:(hc % 32) * 512 + 512]
                    W(f"silu_{tg}_{hc}", f"gmm_{tg}_{hc}")
                    TK(vector.scalar_tensor_tensor(
                        out=hid, in0=bank(2 + hc % 2), scalar=scol(1),
                        in1=A_sc[:, (hc % 2) * 512:(hc % 2) * 512 + 512],
                        op0=ALU.mult, op1=ALU.mult), "vp", 1, f"hid_{tg}_{hc}")
                for oc in range(EC):
                    W(f"downmm_{tg}_{oc}_1")
                    TK(vector.scalar_tensor_tensor(
                        out=xs(oc, tg * 512, 512), in0=bank(4 + oc % 2),
                        scalar=scol(8), in1=xs(oc, tg * 512, 512),
                        op0=ALU.mult, op1=ALU.add), "vp", 1, f"x3_{tg}_{oc}")
                W(f"mrow_1_{tg}", f"sqrow_1_{tg}")
                TK(vector.tensor_mul(rowC, rowA, rowA), "vp", 1)
                TK(vector.tensor_sub(rowB, rowB, rowC), "vp", 1, f"var_1_{tg}")
                W(f"sqrt_1_{tg}")
                TK(vector.reciprocal(out=rowC, in_=rowC), "vp", 1, f"rrow_1_{tg}")
                TK(vector.tensor_mul(rowA, rowA, rowC), "vp", 1)
                TK(vector.tensor_scalar_mul(rowA, rowA, -1.0), "vp", 1, f"nmr_{tg}")
                for tsub in range(4):
                    W(f"colmm_{tg}_{tsub}")
                    TK(vector.tensor_copy(out=ln2col(tsub), in_=bank(6)[:, tsub:tsub + 1]),
                       "vp", 1)
                    TK(vector.tensor_copy(out=ln2col(4 + tsub), in_=bank(7)[:, tsub:tsub + 1]),
                       "vp", 1, f"cols_{tg}_{tsub}")
                for tsub in range(4):
                    grp = tsub % 2
                    W(f"trans_{tg}_{tsub}", f"ostore_{tg}_{tsub - 1}")
                    TK(vector.tensor_scalar(
                        out=A_sc[:, 0:2048], in0=pAll[:, grp * 2048:grp * 2048 + 2048],
                        scalar1=ln2col(tsub), scalar2=ln2col(4 + tsub),
                        op0=ALU.mult, op1=ALU.add), "vp", 1, f"ln2evac_{tg}_{tsub}")

        @block.tensor
        def _(tensor):
            W = lambda *e: C.wait(tensor, "tensor", *e)
            TK = lambda i, sem, n, ev=None: C.tick(i, sem, n, ev)
            W("ones", "xbload", "biasload", "p0done")
            wbuf = lambda b: A_wt[:, b * 4096:(b + 1) * 4096]
            for s, co, tg, g in kq_groups(8):
                W(f"kvw_{s}", f"kevac_{g - 2}")
                for ec in range(EC):
                    tensor.matmul(bank(g % 2),
                                  wbuf(s % 2)[:, ec * 256 + (co - 2 * s) * 128:
                                              ec * 256 + (co - 2 * s) * 128 + 128],
                                  xbs(ec, tg * 512, 512),
                                  start=(ec == 0), stop=False)
                TK(tensor.matmul(bank(g % 2), kb_b(co * 128, 128), ones512_b,
                                 start=False, stop=True),
                   "tp", 1, f"kproj_{g}")
                if co == 2 * s + 1 and tg == 1:
                    C.ev(f"kvslab_done_{s}", "tp")
                W(f"kevac_{g}", f"kt2_{g - 2}")
                TK(tensor.matmul(bank(2 + g % 2), rmatsl,
                                 A_sc[:, (g % 2) * 512:(g % 2) * 512 + 512],
                                 start=True, stop=True),
                   "tp", 1, f"rotmm_K_{g}")
            for vg in range(64):
                s, tc = vg // 8, vg % 8
                W(f"kvw_{8 + s}", f"vevac_{vg - 2}")
                for ec in range(EC):
                    tensor.matmul(bank(vg % 2, 256),
                                  xbs(ec, tc * 128, 128),
                                  wbuf((8 + s) % 2)[:, ec * 256:ec * 256 + 256],
                                  start=(ec == 0), stop=False)
                TK(tensor.matmul(bank(vg % 2, 256), ones128r_b, vb_b(s * 256, 256),
                                 start=False, stop=True),
                   "tp", 1, f"vproj_{vg}")
                if tc == 7:
                    C.ev(f"vslab_done_{8 + s}", "tp")
            for s, co, tg, g in kq_groups(8):
                W(f"qw_{s}", f"qevac_{g - 2}")
                if g < 2:
                    W(f"vevac_{62 + g}")
                for ec in range(EC):
                    tensor.matmul(bank(g % 2),
                                  wbuf(s % 2)[:, ec * 256 + (co - 2 * s) * 128:
                                              ec * 256 + (co - 2 * s) * 128 + 128],
                                  xbs(ec, tg * 512, 512),
                                  start=(ec == 0), stop=False)
                TK(tensor.matmul(bank(g % 2), qb_b(co * 128, 128), ones512_b,
                                 start=False, stop=True),
                   "tp", 1, f"qproj_{g}")
                if co == 2 * s + 1 and tg == 1:
                    C.ev(f"qslab_done_{s}", "tp")
                W(f"qevac_{g}", f"qt2_{g - 2}")
                TK(tensor.matmul(bank(2 + g % 2), rmatsl,
                                 A_sc[:, (g % 2) * 512:(g % 2) * 512 + 512],
                                 start=True, stop=True),
                   "tp", 1, f"rotmm_Q_{g}")
                if g == 31:
                    C.ev("qlastmm", "tp")
            if _PHASE < 2:
                return
            W("p1red")
            TK(tensor.matmul(bank(3)[0:1, 0:3], onescol_f,
                             s_acc[:, 48:51], start=True, stop=True),
               "tp", 1, "sclsum")
            W("sclload", "sclrow")
            TK(tensor.matmul(bank(3)[:, 0:3], ones128r_f, sclback_row[0:1, 0:3],
                             start=True, stop=True), "tp", 1, "thrbcast")
            if _PHASE < 3:
                return
            for qg in range(2 if _PHASE >= 3.9 else 1):
                for ks in range(1 if _PHASE == 3.06 else (16 if _PHASE >= 3.05 else 0)):
                    for tsub in range(2):
                        tc = 2 * ks + tsub
                        if _PHASE == 3.067:
                            W(f"kload_{qg}_{ks}")
                        else:
                            W(f"kload_{qg}_{ks}", f"exp_{qg}_{tc - 2}")
                        for ec in range(EC):
                            mm = tensor.matmul(
                                bank(tc % 2),
                                A_xb[:, (ks % 2) * 4096 + ec * 256 + tsub * 128:
                                     (ks % 2) * 4096 + ec * 256 + tsub * 128 + 128],
                                qslice(ec, qg * 512, 512),
                                start=(ec == 0), stop=(ec == EC - 1))
                        TK(mm, "tp", 1, f"score_{qg}_{tc}")
                        if tsub == 1:
                            C.ev(f"scoreslab_{qg}_{ks}", "tp")
                        if tc >= 1 and _PHASE >= 3.09:
                            W(f"exp_{qg}_{tc - 1}")
                            TK(tensor.matmul(bank(2)[0:1, :], onescol_b,
                                             A_exp[:, (tc - 1) * 512:tc * 512],
                                             start=(tc - 1 == 0), stop=False),
                               "tp", 1, f"summm_{qg}_{tc - 1}")
                if _PHASE >= 3.09:
                    W(f"exp_{qg}_{TC - 1}")
                    TK(tensor.matmul(bank(2)[0:1, :], onescol_b,
                                     A_exp[:, (TC - 1) * 512:TC * 512],
                                     start=False, stop=True),
                       "tp", 1, f"summm_{qg}_{TC - 1}")
                if _PHASE < 3.2:
                    continue
                W(f"recip_{qg}")
                TK(tensor.matmul(bank(0), ones128r_f, rowA, start=True, stop=True),
                   "tp", 1, f"recipbcmm_{qg}")
                for eh in range(2):
                    for tc in range(TC):
                        vs = tc // 2
                        W(f"vload_{qg}_{eh}_{vs}")
                        if tc == 0:
                            W(f"recB_{qg}", f"sumevac_{qg}")
                            if eh == 1:
                                for e8 in range(8):
                                    W(f"avevac_{qg}_0_{e8}")
                        for e8 in range(8):
                            mm = tensor.matmul(
                                bank(e8),
                                A_kv[:, (vs % 2) * 2048 + (tc % 2) * 1024 + e8 * 128:
                                     (vs % 2) * 2048 + (tc % 2) * 1024 + e8 * 128 + 128],
                                A_exp[:, tc * 512:(tc + 1) * 512],
                                start=(tc == 0), stop=(tc == TC - 1))
                        if tc % 2 == 1:
                            TK(mm, "tp", 1, f"avslab_{qg}_{eh}_{vs}")
                    C.ev(f"avdone_{qg}_{eh}", "tp")
                for oc in range(EC if _PHASE >= 3.3 else 0):
                    W(f"owload_{qg}_{oc // 2}", f"oadd_{qg}_{oc - 2}")
                    if oc == 0:
                        for e8 in range(8):
                            W(f"avevac_{qg}_1_{e8}")
                    for ec in range(EC):
                        mm = tensor.matmul(
                            bank(oc % 2),
                            wbuf((oc // 2) % 2)[:, ec * 256 + (oc % 2) * 128:
                                                ec * 256 + (oc % 2) * 128 + 128],
                            A_xb[:, 8192 + ec * 512:8192 + ec * 512 + 512],
                            start=(ec == 0), stop=(ec == EC - 1))
                    TK(mm, "tp", 1, f"oproj_{qg}_{oc}")
                    if oc % 2 == 1:
                        C.ev(f"oproj_slabdone_{qg}_{oc // 2}", "tp")
            if _PHASE < 4:
                return
            W("oadd_1_15")
            for tg in range(2):
                if tg == 1:
                    W("mb_0", "rb_0")
                for ec in range(EC):
                    mm = tensor.matmul(bank(2)[0:1, :], onescol_f, xs(ec, tg * 512, 512),
                                       start=(ec == 0), stop=(ec == EC - 1))
                TK(mm, "tp", 1, f"meanmm_0_{tg}")
                for ec in range(EC):
                    W(f"sq_0_{tg}_{ec}")
                    TK(tensor.matmul(bank(3)[0:1, :], onescol_b,
                                     A_rp[:, (ec % 2) * 512:(ec % 2) * 512 + 512],
                                     start=(ec == 0), stop=(ec == EC - 1)),
                       "tp", 1, f"sqmm_0_{tg}_{ec}")
                W(f"mrow_0_{tg}")
                TK(tensor.matmul(bank(2), ones128r_f, rowA, start=True, stop=True),
                   "tp", 1, f"bcastm_0_{tg}")
                W(f"rrow_0_{tg}")
                TK(tensor.matmul(bank(3), ones128r_f, rowC, start=True, stop=True),
                   "tp", 1, f"bcastr_0_{tg}")
            if _PHASE < 5:
                return
            for tg in range(2):
                W("x2done")
                if tg == 1:
                    W("ln2evac_0_2", "ln2evac_0_3", "cols_0_3")
                for j in range(32):
                    for s01 in range(2):
                        hc = 2 * j + s01
                        W(f"uload_{tg}_{j}", f"silu_{tg}_{hc - 2}")
                        for ec in range(EC):
                            mm = tensor.matmul(
                                bank(hc % 2),
                                A_wt[:, (ec // 2) * 512 + (ec % 2) * 256 + s01 * 128:
                                     (ec // 2) * 512 + (ec % 2) * 256 + s01 * 128 + 128],
                                xbs(ec, tg * 512, 512),
                                start=(ec == 0), stop=(ec == EC - 1))
                        TK(mm, "tp", 1, f"upg_{tg}_{hc}")
                        if s01 == 1:
                            C.ev(f"upmm_{tg}_{j}", "tp")
                        W(f"gload_{tg}_{j}", f"hid_{tg}_{hc - 2}")
                        for ec in range(EC):
                            mm = tensor.matmul(
                                bank(2 + hc % 2),
                                A_wt[:, 4096 + (ec // 2) * 512 + (ec % 2) * 256 + s01 * 128:
                                     4096 + (ec // 2) * 512 + (ec % 2) * 256 + s01 * 128 + 128],
                                xbs(ec, tg * 512, 512),
                                start=(ec == 0), stop=(ec == EC - 1))
                        TK(mm, "tp", 1, f"gmm_{tg}_{hc}")
                        if s01 == 1:
                            C.ev(f"gatemm_{tg}_{j}", "tp")
                W(f"hid_{tg}_{HC - 1}")
                for oc in range(EC):
                    W(f"x3_{tg}_{oc - 1}")
                    for hh in range(2):
                        W(f"dload_{tg}_{oc}_{hh}")
                        for i in range(32):
                            hc = hh * 32 + i
                            hid = (A_q if hc < 32 else A_exp)[
                                :, (hc % 32) * 512:(hc % 32) * 512 + 512]
                            mm = tensor.matmul(
                                bank(4 + oc % 2),
                                A_wt[:, (oc * 2 + hh) % 2 * 4096 + i * 128:
                                     (oc * 2 + hh) % 2 * 4096 + i * 128 + 128],
                                hid,
                                start=(hc == 0), stop=(hc == HC - 1))
                        TK(mm, "tp", 1, f"downmm_{tg}_{oc}_{hh}")
                W(f"x3_{tg}_{EC - 1}")
                for ec in range(EC):
                    mm = tensor.matmul(bank(6)[0:1, :], onescol_f, xs(ec, tg * 512, 512),
                                       start=(ec == 0), stop=(ec == EC - 1))
                TK(mm, "tp", 1, f"meanmm_1_{tg}")
                for ec in range(EC):
                    W(f"sq_1_{tg}_{ec}")
                    TK(tensor.matmul(bank(7)[0:1, :], onescol_b,
                                     A_rp[:, (ec % 2) * 512:(ec % 2) * 512 + 512],
                                     start=(ec == 0), stop=(ec == EC - 1)),
                       "tp", 1, f"sqmm_1_{tg}_{ec}")
                W(f"rrow_1_{tg}", f"nmr_{tg}")
                for tsub in range(4):
                    TK(tensor.matmul(bank(6)[:, tsub:tsub + 1],
                                     rowC[0:1, tsub * 128:tsub * 128 + 128], ones1_f,
                                     start=True, stop=True), "tp", 1)
                    TK(tensor.matmul(bank(7)[:, tsub:tsub + 1],
                                     rowA[0:1, tsub * 128:tsub * 128 + 128], ones1_f,
                                     start=True, stop=True), "tp", 1, f"colmm_{tg}_{tsub}")
                for tsub in range(4):
                    grp = tsub % 2
                    W(f"ln2evac_{tg}_{tsub - 2}", f"cols_{tg}_3")
                    for ec in range(EC):
                        mm = tensor.transpose(
                            pAll[:, grp * 2048 + ec * 128:grp * 2048 + ec * 128 + 128],
                            xs(ec, tg * 512 + tsub * 128, 128), identsl)
                    TK(mm, "tp", 1, f"trans_{tg}_{tsub}")

        @block.gpsimd
        def _(gpsimd):
            W = lambda *e: C.wait(gpsimd, "gpsimd", *e)
            TK = lambda i, sem, n, ev=None: C.tick(i, sem, n, ev)
            TK(gpsimd.dma_start(out=warm_a, in_=qb[0:1, 0:4]), "qR", 16)
            C.wait_total(gpsimd, "gpsimd", "qR")
            TK(gpsimd.collective_compute("AllGather", ALU.bypass,
                                         replica_groups=ALL_GROUP,
                                         ins=[warm_a.opt()], outs=[warm_b.opt()]),
               "cc0", 1, "cc_warm")
            TK(gpsimd.dma_start(
                out=A_xb.ap().rearrange("p (ec t) -> p ec t", t=1024),
                in_=xT.rearrange("(ec p) t -> p ec t", p=128)), "qXB", 16, "xbload")
            TK(gpsimd.dma_start(out=A_exp[0:1, 0:2048], in_=qb), "qR", 16)
            TK(gpsimd.dma_start(out=A_exp[0:1, 2048:4096], in_=kb), "qR", 16)
            TK(gpsimd.dma_start(out=A_exp[0:1, 4096:6144], in_=vb), "qR", 16, "biasload")

            def slab_dma(dst_buf, src, c0, ev):
                TK(gpsimd.dma_start(
                    out=A_wt.ap()[:, dst_buf * 4096:dst_buf * 4096 + 4096]
                        .rearrange("p (ec c) -> p ec c", c=256),
                    in_=src[:, c0:c0 + 256].rearrange("(ec p) c -> p ec c", p=128)),
                   f"qW{dst_buf}", 16, ev)

            slab_src = [(kvwT, s * 256, f"kvw_{s}") for s in range(16)] + \
                       [(qwT, s * 256, f"qw_{s}") for s in range(8)]
            done_ev = [f"kvslab_done_{s}" for s in range(8)] + \
                      [f"vslab_done_{s}" for s in range(8, 16)] + \
                      [f"qslab_done_{s}" for s in range(8)]
            kst = [0]

            def kstores(upto):
                while kst[0] < min(upto, 32):
                    g = kst[0]
                    kst[0] += 1
                    s2, co = g // 4, 2 * (g // 4) + (g % 4) // 2
                    tg = g % 2
                    W(f"krope_{g}")
                    TK(gpsimd.dma_start(
                        out=k_send[co * 128:(co + 1) * 128, tg * 512:(tg + 1) * 512],
                        in_=A_kv[:, (g % 2) * 512:(g % 2) * 512 + 512]),
                       f"qKS{g % 2}", 16, f"kstore_{g}")

            def tile_in_g(i):
                r, jj = i // 16, i % 16
                if r < 2:
                    srcw = (upT_s, gateT_s)[r]
                    rr, cc2 = jj // 8, jj % 8
                    return srcw[rr * 128:(rr + 1) * 128, cc2 * 1024:(cc2 + 1) * 1024]
                return downT_s[jj]

            p1g = [0]

            def p1loads(n):
                for _ in range(n):
                    if _PHASE < 2 or p1g[0] >= 48:
                        return
                    i = p1g[0]
                    p1g[0] += 1
                    W(f"p1_{i - 8}")
                    TK(gpsimd.dma_start(
                        out=A_exp[:, 6656 + (i % 8) * 1024:6656 + (i % 8) * 1024 + 1024],
                        in_=tile_in_g(i)), f"qPA{i % 8}", 16, f"p1load_{i}")

            for j in range(16):
                if 2 <= j < 10:
                    kstores((j - 1) * 4)
                if j == 10:
                    kstores(32)
                    W("kstore_30", "kstore_31")
                    TK(gpsimd.collective_compute("AllGather", ALU.bypass,
                                                 replica_groups=KV_GROUPS,
                                                 ins=[k_send.opt()], outs=[k_full.opt()]),
                       "cc1", 1, "cc_k")
                if j >= 2:
                    W(done_ev[j - 2])
                src, c0, ev = slab_src[j]
                slab_dma(j % 2, src, c0, ev)
                p1loads(2)
            kstores(32)
            for j in range(16, 24):
                W(done_ev[j - 2])
                src, c0, ev = slab_src[j]
                slab_dma(j % 2, src, c0, ev)
                p1loads(1)
            W("vstore_62", "vstore_63")
            TK(gpsimd.collective_compute("AllGather", ALU.bypass,
                                         replica_groups=KV_GROUPS,
                                         ins=[v_send.opt()], outs=[v_full.opt()]),
               "cc2", 1, "cc_v")
            p1loads(48)
            if _PHASE >= 2:
                W("sclstore")
                TK(gpsimd.collective_compute("AllReduce", ALU.add,
                                             replica_groups=ALL_GROUP,
                                             ins=[scl_send.opt()], outs=[scl_full.opt()]),
                   "cc3", 1, "cc_scl")
            if _PHASE >= 2:
                W("qtstore_15")
                TK(gpsimd.collective_compute("AllGather", ALU.bypass,
                                             replica_groups=ALL_GROUP,
                                             ins=[up_send.opt()], outs=[up_full.opt()]),
                   "cc4", 1, "cc_up")
                W("qtstore_31")
                TK(gpsimd.collective_compute("AllGather", ALU.bypass,
                                             replica_groups=ALL_GROUP,
                                             ins=[gate_send.opt()], outs=[gate_full.opt()]),
                   "cc5", 1, "cc_gate")
                W("qtstore_47")
                TK(gpsimd.collective_compute("AllGather", ALU.bypass,
                                             replica_groups=ALL_GROUP,
                                             ins=[down_send.opt()], outs=[down_full.opt()]),
                   "cc6", 1, "cc_down")
            if _PHASE >= 3.3:
                for qg in range(2 if _PHASE >= 3.9 else 1):
                    for s in range(8):
                        W(f"oproj_slabdone_{qg}_{s - 2}" if s >= 2 else
                          (f"oproj_slabdone_{qg - 1}_6" if qg == 1 else "qslab_done_7"))
                        if qg == 1 and s == 1:
                            W("oproj_slabdone_0_7")
                        slab_dma(s % 2, owT, s * 256, f"owload_{qg}_{s}")
            if _PHASE == 3.06:
                W("dbg2")
                TK(gpsimd.dma_start(out=out_s[0:128, 0:1024], in_=A_sc[:, 0:1024]),
                   "qOS", 16, "dbgstore")
            if _PHASE in (3.95, 4.5):
                W("oadd_1_15" if _PHASE == 3.95 else "x2done")
                TK(gpsimd.dma_start(out=out_s[0:128, :], in_=A_x[:, 0:2048]),
                   "qOS", 16, "dbgx")
            if _PHASE == 2.5:
                W("cc_up")
                TK(gpsimd.dma_start(out=out_s[0:128, :],
                                    in_=up_full[0, 0, :, 0:2048]), "qOS", 16, "dbgu")
            for tg in range(2 if (_PHASE >= 5 and _PHASE != 5.5) else 0):
                for tsub in range(4):
                    W(f"ln2evac_{tg}_{tsub}")
                    TK(gpsimd.dma_start(
                        out=out_s[tg * 512 + tsub * 128:tg * 512 + tsub * 128 + 128, :],
                        in_=A_sc[:, 0:2048]), "qOS", 16, f"ostore_{tg}_{tsub}")
            if _PHASE == 5.5:
                W("x3_1_15")
                TK(gpsimd.dma_start(out=out_s[0:128, :], in_=A_x[:, 0:2048]),
                   "qOS", 16, "dbgx3")
            if _PHASE == 5.6:
                W("x3_1_15")
                TK(gpsimd.dma_start(out=out_dbg, in_=A_x[:, :]), "qOS", 16, "dbgx3f")
            if _PHASE == 5.7:
                W("x3_1_15")
                TK(gpsimd.dma_start(out=out_dbg[:, 0:2048], in_=s_mf[:, :]),
                   "qOS", 16, "dbg57a")
                TK(gpsimd.dma_start(out=out_dbg[:, 2048:4096], in_=A_q[:, 0:2048]),
                   "qOS", 16, "dbg57b")
                TK(gpsimd.dma_start(out=out_dbg[:, 4096:6144], in_=A_exp[:, 0:2048]),
                   "qOS", 16, "dbg57c")
                TK(gpsimd.dma_start(out=out_dbg[:, 6144:8192], in_=A_xb[:, 0:2048]),
                   "qOS", 16, "dbg57d")
            C.wait_total(gpsimd, "gpsimd", "qOS")

    return nc, C.events


_CACHE = {}


def _get_nc():
    if "nc" not in _CACHE:
        _, events = _build(None)
        nc, events2 = _build(events)
        _CACHE["nc"] = nc
    return _CACHE["nc"]


def _rope_tables():
    half = 64
    inv = 1.0 / (10000.0 ** (np.arange(half, dtype=np.float64) / half))
    pos = np.arange(S, dtype=np.float64)
    ang = pos[None, :] * inv[:, None]
    ang = np.concatenate([ang, ang], axis=0)
    cos, sin = np.cos(ang), np.sin(ang)
    out = []
    for t64 in (cos, sin):
        hi = t64.astype(np.float32)
        lo = (t64 - hi.astype(np.float64)).astype(np.float32)
        out.append((hi, lo))
    return out


def kernel(L0_attn_x, L0_attn_q_w, L0_attn_q_b, L0_attn_kv_w, L0_attn_kv_b,
           L0_attn_o_w, L0_ffn_W_up, L0_ffn_W_gate, L0_ffn_W_down):
    f32 = np.float32
    x = np.asarray(L0_attn_x, f32)
    qwT = np.ascontiguousarray(np.asarray(L0_attn_q_w, f32).T)
    kvwT = np.ascontiguousarray(np.asarray(L0_attn_kv_w, f32).T)
    owT = np.ascontiguousarray(np.asarray(L0_attn_o_w, f32).T)
    qb = np.ascontiguousarray(np.asarray(L0_attn_q_b, f32).reshape(1, E))
    kvb = np.asarray(L0_attn_kv_b, f32)
    kb = np.ascontiguousarray(kvb[:E].reshape(1, E))
    vb = np.ascontiguousarray(kvb[E:].reshape(1, E))
    upT = np.ascontiguousarray(np.asarray(L0_ffn_W_up, f32).T)
    gateT = np.ascontiguousarray(np.asarray(L0_ffn_W_gate, f32).T)
    downT = np.ascontiguousarray(np.asarray(L0_ffn_W_down, f32).T)

    (cos_hi, cos_lo), (sin_hi, sin_lo) = _rope_tables()
    R = np.zeros((128, 128), f32)
    for d in range(64):
        R[d + 64, d] = -1.0
        R[d, d + 64] = 1.0
    ident = np.eye(128, dtype=f32)

    nc = _get_nc()
    in_maps = []
    for c in range(NCORE):
        b, q4 = c // 4, c % 4
        t0 = q4 * T
        xT_c = np.ascontiguousarray(x[b, t0:t0 + T, :].T)
        down_sl = downT[c * 1024:(c + 1) * 1024, :]
        down_tiled = np.ascontiguousarray(
            down_sl.reshape(8, 128, 16, 128).transpose(2, 1, 0, 3).reshape(16, 128, 1024))
        in_maps.append(dict(
            xT=xT_c, qwT=qwT, kvwT=kvwT, owT=owT, qb=qb, kb=kb, vb=vb,
            upT_s=np.ascontiguousarray(upT[c * 256:(c + 1) * 256, :]),
            gateT_s=np.ascontiguousarray(gateT[c * 256:(c + 1) * 256, :]),
            downT_s=down_tiled,
            cos_hi=np.ascontiguousarray(cos_hi[:, t0:t0 + T]),
            cos_lo=np.ascontiguousarray(cos_lo[:, t0:t0 + T]),
            sin_hi=np.ascontiguousarray(sin_hi[:, t0:t0 + T]),
            sin_lo=np.ascontiguousarray(sin_lo[:, t0:t0 + T]),
            rmat=R, ident=ident))

    res = run_bass_kernel_spmd(nc, in_maps, core_ids=list(range(NCORE)))
    out = np.empty((B, S, E), f32)
    for c in range(NCORE):
        out[c // 4, (c % 4) * T:(c % 4 + 1) * T, :] = res.results[c]["out"]
    return out


# version 26
# speedup vs baseline: 1.0349x; 1.0346x over previous
import numpy as np
from contextlib import ExitStack

import concourse.bass as bass
import concourse.mybir as mybir
from concourse.bass_utils import run_bass_kernel_spmd

F32 = mybir.dt.float32
BF16 = mybir.dt.bfloat16
AF = mybir.ActivationFunctionType
ALU = mybir.AluOpType

B, S, E, H = 2, 4096, 2048, 8192
T = 1024
EC = 16
TC = 32
HC = 64
NCORE = 8
HS = float(128) ** -0.5
CEXP = 12.0
NW = float(E * H)
EPS_T = 1e-8
EPS_LN = 1e-5

KV_GROUPS = [[0, 1, 2, 3], [4, 5, 6, 7]]
ALL_GROUP = [list(range(8))]

SEMS = ["qX", "qR", "qB", "qXB", "qK0", "qK1", "qV0", "qV1", "qU", "qGt",
        "qD0", "qD1", "qW0", "qW1", "qKS0", "qKS1", "qVS0", "qVS1",
        "qP0", "qP1", "qPA0", "qPA1", "qPA2", "qPA3", "qPA4", "qPA5", "qPA6", "qPA7", "qQSu", "qQSg", "qQSd", "qOS",
        "tp", "vp", "ap",
        "cc0", "cc1", "cc2", "cc3", "cc4", "cc5", "cc6"]


class _Ctx:

    def __init__(self, prev, handles):
        self.real = prev is not None
        self.prev = prev or {}
        self.h = handles
        self.counts = {s: 0 for s in SEMS}
        self.events = {}
        self.waited = {}

    def tick(self, instr, sem, n, ev=None):
        self.counts[sem] += n
        if self.real:
            instr.then_inc(self.h[sem], n)
        if ev is not None:
            assert ev not in self.events, f"duplicate event {ev}"
            self.events[ev] = (sem, self.counts[sem])
            if self.real:
                assert self.prev[ev] == self.events[ev], (
                    f"pass drift at {ev}: {self.prev[ev]} vs {self.events[ev]}")
        return instr

    def ev(self, ev, sem):
        assert ev not in self.events, f"duplicate event {ev}"
        self.events[ev] = (sem, self.counts[sem])
        if self.real:
            assert self.prev[ev] == self.events[ev]

    def wait(self, eng, engname, *evs):
        if not self.real:
            return
        for evname in evs:
            if evname is None:
                continue
            if evname not in self.prev:
                assert "-" in evname or _PHASE < 99, f"unknown event {evname}"
                continue
            sem, val = self.prev[evname]
            if val <= 0:
                continue
            key = (engname, sem)
            if self.waited.get(key, -1) >= val:
                continue
            self.waited[key] = val
            eng.wait_ge(self.h[sem], val)

    def wait_total(self, eng, engname, sem):
        val = self.counts[sem]
        if not self.real or val <= 0:
            return
        key = (engname, sem)
        if self.waited.get(key, -1) >= val:
            return
        self.waited[key] = val
        eng.wait_ge(self.h[sem], val)


_PHASE = 99


def _build(prev=None):
    nc = bass.Bass("TRN2", target_bir_lowering=False,
                   detect_race_conditions=False)
    dt_in = {}

    def din(name, shape):
        dt_in[name] = nc.dram_tensor(name, shape, F32, kind="ExternalInput").ap()
        return dt_in[name]

    xT = din("xT", [E, T])
    qwT = din("qwT", [E, E])
    kvwT = din("kvwT", [E, 2 * E])
    owT = din("owT", [E, E])
    qb = din("qb", [1, E])
    kb = din("kb", [1, E])
    vb = din("vb", [1, E])
    upT_s = din("upT_s", [256, H])
    gateT_s = din("gateT_s", [256, H])
    downT_s = din("downT_s", [16, 128, 1024])
    cos_hi = din("cos_hi", [128, T])
    cos_lo = din("cos_lo", [128, T])
    sin_hi = din("sin_hi", [128, T])
    sin_lo = din("sin_lo", [128, T])
    rmat = din("rmat", [128, 128])
    ident = din("ident", [128, 128])

    out_s = nc.dram_tensor("out", [T, E], F32, kind="ExternalOutput").ap()
    out_dbg = nc.dram_tensor("out_dbg", [128, 16384], F32, kind="ExternalOutput").ap()

    k_send = nc.dram_tensor("k_send", [E, T], BF16).ap()
    v_send = nc.dram_tensor("v_send", [T, E], BF16).ap()
    k_full = nc.dram_tensor("k_full", [4, EC, 128, T], BF16).ap()
    v_full = nc.dram_tensor("v_full", [16, 2, 128, E], BF16).ap()
    up_send = nc.dram_tensor("up_send", [256, H], BF16).ap()
    gate_send = nc.dram_tensor("gate_send", [256, H], BF16).ap()
    down_send = nc.dram_tensor("down_send", [16, 128, 1024], BF16).ap()
    up_full = nc.dram_tensor("up_full", [8, 2, 128, H], BF16).ap()
    gate_full = nc.dram_tensor("gate_full", [8, 2, 128, H], BF16).ap()
    down_full = nc.dram_tensor("down_full", [8, 16, 128, 1024], BF16).ap()
    scl_send = nc.dram_tensor("scl_send", [1, 4], F32).ap()
    scl_full = nc.dram_tensor("scl_full", [1, 4], F32).ap()
    warm_a = nc.dram_tensor("warm_a", [1, 4], F32).ap()
    warm_b = nc.dram_tensor("warm_b", [8, 4], F32).ap()

    with ExitStack() as ctx:
        sb = lambda n, s, dt: ctx.enter_context(nc.sbuf_tensor(n, s, dt))
        A_x = sb("A_x", [128, 16384], F32)
        A_xb = sb("A_xb", [128, 16384], BF16)
        A_q = sb("A_q", [128, 16384], BF16)
        A_exp = sb("A_exp", [128, 16384], BF16)
        A_kv = sb("A_kv", [128, 4096], BF16)
        A_wt = sb("A_wt", [128, 8192], BF16)
        A_rp = sb("A_rp", [128, 2048], BF16)
        A_sc = sb("A_sc", [128, 2048], F32)
        s_qt = sb("s_qt", [128, 1024], BF16)
        s_sm = sb("s_sm", [128, 648], BF16)
        s_mf = sb("s_mf", [128, 2048], F32)
        s_acc = sb("s_acc", [128, 52], F32)
        pAll = ctx.enter_context(nc.psum_tensor("pAll", [128, 4096], F32))

        handles = {s: ctx.enter_context(nc.semaphore(s)) for s in SEMS} \
            if prev is not None else {s: None for s in SEMS}
        C = _Ctx(prev, handles)
        block = ctx.enter_context(nc.Block())

        bank = lambda i, w=512: pAll[:, i * 512:i * 512 + w]

        def xs(ec, lo, n):
            return A_x[:, ec * 1024 + lo: ec * 1024 + lo + n]

        def xbs(ec, lo, n):
            return A_xb[:, ec * 1024 + lo: ec * 1024 + lo + n]

        def qslice(ec, lo, n):
            return A_q[:, ec * 1024 + lo: ec * 1024 + lo + n]

        ones512_b = s_sm[0:1, 0:512]
        ones128r_b = s_sm[0:1, 512:640]
        onescol_b = s_sm[:, 640:641]
        onescol_f = s_mf[:, 0:1]
        ones1_f = s_mf[0:1, 1:2]
        ones128r_f = s_mf[0:1, 2:130]
        scol = lambda j: s_mf[:, 130 + j:131 + j]
        srow = s_mf[0:1, 162:166]
        sclback_row = s_mf[0:1, 166:170]
        drow = s_mf[0:1, 150:158]
        rowA = s_mf[0:1, 178:690]
        recB = s_mf[:, 690:1202]
        rowB = s_mf[0:1, 690:1202]
        identsl = s_mf[:, 1202:1330]
        rowC = s_mf[0:1, 1330:1842]
        ln2col = lambda j: s_mf[:, 1842 + j:1843 + j]
        rmatsl = s_mf[:, 1858:1986]
        c_negC = s_mf[:, 1986:1987]
        c_eps = s_mf[:, 1987:1988]

        qb_b = lambda lo, n: A_exp[0:1, lo:lo + n]
        kb_b = lambda lo, n: A_exp[0:1, 2048 + lo:2048 + lo + n]
        vb_b = lambda lo, n: A_exp[0:1, 4096 + lo:4096 + lo + n]

        cos_sl = lambda tg: A_rp[:, tg * 512:(tg + 1) * 512]
        sin_sl = lambda tg: A_rp[:, 1024 + tg * 512:1024 + tg * 512 + 512]

        def kq_groups(nslab):
            for s in range(nslab):
                for ci in range(2):
                    for tg in range(2):
                        yield s, 2 * s + ci, tg, s * 4 + ci * 2 + tg

        @block.sync
        def _(sync):
            W = lambda *e: C.wait(sync, "sync", *e)
            TK = lambda i, sem, n, ev=None: C.tick(i, sem, n, ev)

            TK(sync.dma_start(out=identsl, in_=ident), "qX", 16)
            TK(sync.dma_start(out=rmatsl, in_=rmat), "qX", 16)
            TK(sync.dma_start(
                out=A_x.ap().rearrange("p (ec t) -> p ec t", t=1024),
                in_=xT.rearrange("(ec p) t -> p ec t", p=128)), "qX", 16, "xload")
            TK(sync.dma_start(out=A_sc[:, 0:1024], in_=cos_hi), "qX", 16)
            TK(sync.dma_start(out=A_sc[:, 1024:2048], in_=cos_lo), "qX", 16, "coshl")
            W("cosdone")
            TK(sync.dma_start(out=A_sc[:, 0:1024], in_=sin_hi), "qX", 16)
            TK(sync.dma_start(out=A_sc[:, 1024:2048], in_=sin_lo), "qX", 16, "sinhl")
            C.ev("p0done", "qX")

            if _PHASE < 3:
                return
            for qg in range(2 if _PHASE >= 3.9 else 1):
                W("cc_k")
                W("qlastmm")
                for ks in range(16):
                    a = qg * 16 + ks - 2
                    W(f"scoreslab_{a // 16}_{a % 16}" if a >= 0 else None)
                    rank, tl0 = ks // 4, (ks % 4) * 256
                    TK(sync.dma_start(
                        out=A_xb.ap()[:, (ks % 2) * 4096:(ks % 2) * 4096 + 4096]
                            .rearrange("p (ec t) -> p ec t", t=256),
                        in_=k_full[rank][:, :, tl0:tl0 + 256]
                            .transpose([1, 0, 2])),
                       f"qK{ks % 2}", 16, f"kload_{qg}_{ks}")
                if _PHASE < 3.2:
                    continue
                W("cc_v")
                for eh in range(2):
                    for vs in range(16):
                        a = (qg * 2 + eh) * 16 + vs - 2
                        W(f"avslab_{a // 32}_{(a % 32) // 16}_{a % 16}" if a >= 0 else None)
                        TK(sync.dma_start(
                            out=A_kv.ap()[:, (vs % 2) * 2048:(vs % 2) * 2048 + 2048]
                                .rearrange("p (c e) -> p c e", e=1024),
                            in_=v_full[vs][:, :, eh * 1024:(eh + 1) * 1024]
                                .transpose([1, 0, 2])),
                           f"qV{vs % 2}", 16, f"vload_{qg}_{eh}_{vs}")

            if _PHASE < 5:
                return
            for tg in range(2):
                W("cc_up")
                W("cc_gate")
                if tg == 0:
                    W("oproj_slabdone_1_6", "oproj_slabdone_1_7")
                if tg == 1:
                    W("downmm_0_15_1")
                for j in range(32):
                    W(f"upmm_{tg}_{j - 1}")
                    for rl in range(2):
                        TK(sync.dma_start(
                            out=A_wt.ap()[:, 0:4096]
                                .rearrange("p (r rl c) -> p r rl c", rl=2, c=256)[:, :, rl, :],
                            in_=up_full[:, rl, :, j * 256:(j + 1) * 256]
                                .rearrange("r p c -> p r c")),
                           "qU", 16, f"uload_{tg}_{j}" if rl == 1 else None)
                    W(f"gatemm_{tg}_{j - 1}")
                    for rl in range(2):
                        TK(sync.dma_start(
                            out=A_wt.ap()[:, 4096:8192]
                                .rearrange("p (r rl c) -> p r rl c", rl=2, c=256)[:, :, rl, :],
                            in_=gate_full[:, rl, :, j * 256:(j + 1) * 256]
                                .rearrange("r p c -> p r c")),
                           "qGt", 16, f"gload_{tg}_{j}" if rl == 1 else None)
                W("cc_down")
                for oc in range(16):
                    for hh in range(2):
                        d = oc * 2 + hh
                        if d == 0:
                            W(f"upmm_{tg}_31")
                        elif d == 1:
                            W(f"gatemm_{tg}_31")
                        else:
                            W(f"downmm_{tg}_{(d - 2) // 2}_{(d - 2) % 2}")
                        TK(sync.dma_start(
                            out=A_wt.ap()[:, (d % 2) * 4096:(d % 2) * 4096 + 4096]
                                .rearrange("p (r f) -> p r f", f=1024),
                            in_=down_full[hh * 4:(hh + 1) * 4, oc, :, :]
                                .rearrange("r p f -> p r f")),
                           f"qD{(oc * 2 + hh) % 2}", 16, f"dload_{tg}_{oc}_{hh}")

        @block.scalar
        def _(scalar):
            W = lambda *e: C.wait(scalar, "scalar", *e)
            TK = lambda i, sem, n, ev=None: C.tick(i, sem, n, ev)
            for s, co, tg, g in kq_groups(8):
                W(f"kproj_{g}", f"krope_{g - 2}")
                TK(scalar.activation(out=A_sc[:, (g % 2) * 512:(g % 2) * 512 + 512],
                                     in_=bank(g % 2), func=AF.Copy),
                   "ap", 1, f"kevac_{g}")
            for vg in range(64):
                W(f"vevac_{vg}")
                s, tc = vg // 8, vg % 8
                TK(scalar.dma_start(
                    out=v_send[tc * 128:(tc + 1) * 128, s * 256:(s + 1) * 256],
                    in_=A_kv[:, 1024 + (vg % 2) * 256:1024 + (vg % 2) * 256 + 256]),
                   f"qVS{vg % 2}", 16, f"vstore_{vg}")
            for s, co, tg, g in kq_groups(8):
                W(f"qproj_{g}", f"qrope_{g - 2}")
                TK(scalar.activation(out=A_sc[:, (g % 2) * 512:(g % 2) * 512 + 512],
                                     in_=bank(g % 2), func=AF.Copy, scale=HS),
                   "ap", 1, f"qevac_{g}")
            if _PHASE < 2:
                return
            def tile_in(i):
                r, j = i // 16, i % 16
                src = (upT_s, gateT_s)[r] if r < 2 else None
                if r < 2:
                    rr, cc2 = j // 8, j % 8
                    return src[rr * 128:(rr + 1) * 128, cc2 * 1024:(cc2 + 1) * 1024]
                return downT_s[j]

            def tile_out(i):
                r, j = i // 16, i % 16
                dst = (up_send, gate_send)[r] if r < 2 else None
                if r < 2:
                    rr, cc2 = j // 8, j % 8
                    return dst[rr * 128:(rr + 1) * 128, cc2 * 1024:(cc2 + 1) * 1024]
                return down_send[j]

            W("qdone")
            W("sclrow")
            TK(scalar.dma_start(out=scl_send, in_=srow), "qB", 16, "sclstore")
            W("cc_scl")
            TK(scalar.dma_start(out=sclback_row, in_=scl_full), "qB", 16, "sclload")
            qacts = []
            for _i in range(48):
                qacts.append(("L", _i))
                if _i >= 2:
                    qacts.append(("S", _i - 2))
            qacts += [("S", 46), ("S", 47)]
            qpos = [0]

            def quant_dmas(n):
                for _ in range(n):
                    if qpos[0] >= len(qacts):
                        return
                    kind, i = qacts[qpos[0]]
                    qpos[0] += 1
                    if kind == "L":
                        W(f"quant_{i - 2}")
                        TK(scalar.dma_start(
                            out=A_sc[:, (i % 2) * 1024:(i % 2) * 1024 + 1024],
                            in_=tile_in(i)), f"qP{i % 2}", 16, f"qtload_{i}")
                    else:
                        W(f"quant_{i}")
                        TK(scalar.dma_start(out=tile_out(i), in_=s_qt[:, 0:1024]),
                           ("qQSu", "qQSg", "qQSd")[i // 16], 16, f"qtstore_{i}")

            quant_dmas(len(qacts))

            if _PHASE < 3:
                return
            if _PHASE == 3.06:
                W("score_0_0")
                TK(scalar.activation(out=A_sc[:, 0:512], in_=bank(0), func=AF.Copy),
                   "ap", 1, "dbg1")
                W("score_0_1")
                TK(scalar.activation(out=A_sc[:, 512:1024], in_=bank(1), func=AF.Copy),
                   "ap", 1, "dbg2")
            for qg in range(2 if _PHASE >= 3.9 else 1):
                for tc in range((4 if _PHASE == 3.066 else TC) if _PHASE >= 3.065 else 0):
                    W(f"score_{qg}_{tc}")
                    if qg == 1 and tc == 0:
                        W("avdone_0_1")
                    if _PHASE == 3.065:
                        TK(scalar.activation(out=A_exp[:, tc * 512:(tc + 1) * 512],
                                             in_=bank(tc % 2), func=AF.Copy),
                           "ap", 1, f"exp_{qg}_{tc}")
                    else:
                        TK(scalar.activation(out=A_exp[:, tc * 512:(tc + 1) * 512],
                                             in_=bank(tc % 2), func=AF.Exp,
                                             bias=c_negC),
                           "ap", 1, f"exp_{qg}_{tc}")
                if _PHASE >= 3.09:
                    W(f"summm_{qg}_{TC - 1}")
                    TK(scalar.activation(out=rowA, in_=bank(2)[0:1, :], func=AF.Copy),
                       "ap", 1, f"sumevac_{qg}")
            if _PHASE < 4:
                return
            W("oadd_1_15")
            for tg in range(2):
                for ec in range(EC):
                    W(f"sqmm_0_{tg}_{ec - 2}")
                    TK(scalar.activation(out=A_rp[:, (ec % 2) * 512:(ec % 2) * 512 + 512],
                                         in_=xs(ec, tg * 512, 512), func=AF.Square),
                       "ap", 1, f"sq_0_{tg}_{ec}")
                W(f"meanmm_0_{tg}")
                TK(scalar.activation(out=rowA, in_=bank(2)[0:1, :], func=AF.Copy,
                                     scale=1.0 / E), "ap", 1, f"mrow_0_{tg}")
                W(f"sqmm_0_{tg}_{EC - 1}")
                TK(scalar.activation(out=rowB, in_=bank(3)[0:1, :], func=AF.Copy,
                                     scale=1.0 / E), "ap", 1, f"sqrow_0_{tg}")
                W(f"var_0_{tg}")
                TK(scalar.activation(out=rowC, in_=rowB, func=AF.Sqrt,
                                     bias=c_eps[0:1, :]), "ap", 1, f"sqrt_0_{tg}")

            if _PHASE < 5:
                return
            for tg in range(2):
                W("x2done", f"ostore_{tg - 1}_3")
                for hc in range(HC):
                    W(f"upg_{tg}_{hc}", f"hid_{tg}_{hc - 2}")
                    TK(scalar.activation(out=A_sc[:, (hc % 2) * 512:(hc % 2) * 512 + 512],
                                         in_=bank(hc % 2), func=AF.Silu,
                                         scale=scol(0)), "ap", 1, f"silu_{tg}_{hc}")
                for ec in range(EC):
                    W(f"x3_{tg}_{ec}")
                    W(f"sqmm_1_{tg}_{ec - 2}")
                    TK(scalar.activation(out=A_rp[:, (ec % 2) * 512:(ec % 2) * 512 + 512],
                                         in_=xs(ec, tg * 512, 512), func=AF.Square),
                       "ap", 1, f"sq_1_{tg}_{ec}")
                W(f"meanmm_1_{tg}")
                TK(scalar.activation(out=rowA, in_=bank(6)[0:1, :], func=AF.Copy,
                                     scale=1.0 / E), "ap", 1, f"mrow_1_{tg}")
                W(f"sqmm_1_{tg}_{EC - 1}")
                TK(scalar.activation(out=rowB, in_=bank(7)[0:1, :], func=AF.Copy,
                                     scale=1.0 / E), "ap", 1, f"sqrow_1_{tg}")
                W(f"var_1_{tg}")
                TK(scalar.activation(out=rowC, in_=rowB, func=AF.Sqrt,
                                     bias=c_eps[0:1, :]), "ap", 1, f"sqrt_1_{tg}")

        @block.vector
        def _(vector):
            W = lambda *e: C.wait(vector, "vector", *e)
            TK = lambda i, sem, n, ev=None: C.tick(i, sem, n, ev)
            W("coshl")
            TK(vector.tensor_add(A_rp[:, 0:1024], A_sc[:, 0:1024], A_sc[:, 1024:2048]),
               "vp", 1, "cosdone")
            W("sinhl")
            TK(vector.tensor_add(A_rp[:, 1024:2048], A_sc[:, 0:1024], A_sc[:, 1024:2048]),
               "vp", 1, "sindone")
            vector.memset(ones512_b, 1.0)
            vector.memset(ones128r_b, 1.0)
            vector.memset(onescol_b, 1.0)
            vector.memset(onescol_f, 1.0)
            vector.memset(ones1_f, 1.0)
            vector.memset(srow, 0.0)
            vector.memset(c_negC, -CEXP)
            vector.memset(c_eps, EPS_LN)
            TK(vector.memset(ones128r_f, 1.0), "vp", 1, "ones")
            p1i = [0]

            def p1slot():
                if _PHASE < 2 or p1i[0] >= 48:
                    return
                i = p1i[0]
                p1i[0] += 1
                W(f"p1load_{i}")
                TK(vector.tensor_reduce(
                    out=s_acc[:, i:i + 1],
                    in_=A_exp[:, 6656 + (i % 8) * 1024:6656 + (i % 8) * 1024 + 1024],
                    axis=mybir.AxisListType.X, op=ALU.add,
                    apply_absolute_value=True), "vp", 1, f"p1_{i}")

            def p1drain(n):
                for _ in range(n):
                    if p1i[0] >= 48:
                        return
                    p1slot()

            for s, co, tg, g in kq_groups(8):
                W(f"rotmm_K_{g}")
                TK(vector.scalar_tensor_tensor(
                    out=A_sc[:, 1024 + (g % 2) * 512:1024 + (g % 2) * 512 + 512],
                    in0=bank(2 + g % 2), scalar=1.0, in1=sin_sl(tg),
                    op0=ALU.mult, op1=ALU.mult), "vp", 1, f"kt2_{g}")
                ktmp = A_sc[:, (g % 2) * 512:(g % 2) * 512 + 512]
                vector.tensor_mul(ktmp, ktmp, cos_sl(tg))
                W(f"kstore_{g - 2}")
                TK(vector.tensor_add(
                    A_kv[:, (g % 2) * 512:(g % 2) * 512 + 512], ktmp,
                    A_sc[:, 1024 + (g % 2) * 512:1024 + (g % 2) * 512 + 512]),
                   "vp", 1, f"krope_{g}")
                if g % 2 == 1:
                    p1slot()
            for vg in range(64):
                W(f"vproj_{vg}", f"vstore_{vg - 2}")
                TK(vector.tensor_copy(
                    out=A_kv[:, 1024 + (vg % 2) * 256:1024 + (vg % 2) * 256 + 256],
                    in_=bank(vg % 2, 256)), "vp", 1, f"vevac_{vg}")
                if vg % 4 == 0:
                    p1slot()
            for s, co, tg, g in kq_groups(8):
                W(f"rotmm_Q_{g}")
                TK(vector.scalar_tensor_tensor(
                    out=A_sc[:, 1024 + (g % 2) * 512:1024 + (g % 2) * 512 + 512],
                    in0=bank(2 + g % 2), scalar=1.0, in1=sin_sl(tg),
                    op0=ALU.mult, op1=ALU.mult), "vp", 1, f"qt2_{g}")
                qtmp = A_sc[:, (g % 2) * 512:(g % 2) * 512 + 512]
                vector.tensor_mul(qtmp, qtmp, cos_sl(tg))
                TK(vector.tensor_add(
                    qslice(co, tg * 512, 512), qtmp,
                    A_sc[:, 1024 + (g % 2) * 512:1024 + (g % 2) * 512 + 512]),
                   "vp", 1, f"qrope_{g}")
                if g % 4 == 3:
                    p1slot()
                if g == 31:
                    C.ev("qdone", "vp")
            if _PHASE < 2:
                return
            p1drain(48)
            for r in range(3):
                TK(vector.reduce_sum(out=s_acc[:, 48 + r:49 + r],
                                     in_=s_acc[:, r * 16:(r + 1) * 16],
                                     axis=mybir.AxisListType.X),
                   "vp", 1, "p1red" if r == 2 else None)
            W("sclsum")
            TK(vector.tensor_copy(out=srow[0:1, 0:3], in_=bank(3)[0:1, 0:3]),
               "vp", 1, "sclrow")
            W("thrbcast")
            TK(vector.tensor_scalar(out=s_mf[:, 132:135], in0=bank(3)[:, 0:3],
                                    scalar1=0.5 / NW, scalar2=0.5 * EPS_T,
                                    op0=ALU.mult, op1=ALU.add), "vp", 1)
            TK(vector.tensor_scalar(out=s_mf[:, 135:138], in0=bank(3)[:, 0:3],
                                    scalar1=-0.5 / NW, scalar2=-0.5 * EPS_T,
                                    op0=ALU.mult, op1=ALU.add), "vp", 1)
            TK(vector.tensor_scalar(out=s_mf[:, 130:131], in0=bank(3)[:, 0:1],
                                    scalar1=-1.0 / NW, scalar2=-EPS_T,
                                    op0=ALU.mult, op1=ALU.add), "vp", 1)
            TK(vector.tensor_scalar(out=s_mf[:, 131:132], in0=bank(3)[:, 1:2],
                                    scalar1=1.0 / NW, scalar2=EPS_T,
                                    op0=ALU.mult, op1=ALU.add), "vp", 1)
            TK(vector.tensor_scalar(out=s_mf[:, 138:139], in0=bank(3)[:, 2:3],
                                    scalar1=1.0 / NW, scalar2=EPS_T,
                                    op0=ALU.mult, op1=ALU.add),
               "vp", 1, "scols")
            vqi = [0]

            def vquant(n):
                for _ in range(n):
                    if vqi[0] >= 48:
                        return
                    i = vqi[0]
                    vqi[0] += 1
                    r = i // 16
                    wtile = A_sc[:, (i % 2) * 1024:(i % 2) * 1024 + 1024]
                    W(f"qtload_{i}", f"qtstore_{i - 1}")
                    TK(vector.tensor_single_scalar(out=s_qt[:, 0:1024], in_=wtile,
                                                   scalar=scol(2 + r), op=ALU.is_gt),
                       "vp", 1)
                    TK(vector.scalar_tensor_tensor(out=s_qt[:, 0:1024], in0=wtile,
                                                   scalar=scol(5 + r), in1=s_qt[:, 0:1024],
                                                   op0=ALU.is_lt, op1=ALU.subtract),
                       "vp", 1, f"quant_{i}")

            vquant(48)
            if _PHASE < 3:
                return
            for qg in range(2 if _PHASE >= 3.9 else 1):
                if _PHASE < 3.2:
                    continue
                W(f"sumevac_{qg}")
                TK(vector.reciprocal(out=rowA, in_=rowA), "vp", 1, f"recip_{qg}")
                W(f"recipbcmm_{qg}")
                TK(vector.tensor_copy(out=recB, in_=bank(0)), "vp", 1, f"recB_{qg}")
                for eh in range(2):
                    W(f"avdone_{qg}_{eh}")
                    if qg == 1:
                        W("oproj_0_15")
                    for e8 in range(8):
                        TK(vector.scalar_tensor_tensor(
                            out=A_xb[:, 8192 + (eh * 8 + e8) * 512:
                                     8192 + (eh * 8 + e8) * 512 + 512],
                            in0=bank(e8), scalar=1.0, in1=recB,
                            op0=ALU.mult, op1=ALU.mult),
                           "vp", 1, f"avevac_{qg}_{eh}_{e8}")
                for oc in range(EC if _PHASE >= 3.3 else 0):
                    W(f"oproj_{qg}_{oc}", "p0done")
                    TK(vector.tensor_add(xs(oc, qg * 512, 512), bank(oc % 2),
                                         xs(oc, qg * 512, 512)),
                       "vp", 1, f"oadd_{qg}_{oc}")
            if _PHASE < 4:
                return
            for tg in range(2):
                W(f"mrow_0_{tg}", f"sqrow_0_{tg}")
                TK(vector.tensor_mul(rowC, rowA, rowA), "vp", 1)
                TK(vector.tensor_sub(rowB, rowB, rowC), "vp", 1, f"var_0_{tg}")
                W(f"sqrt_0_{tg}")
                TK(vector.reciprocal(out=rowC, in_=rowC), "vp", 1, f"rrow_0_{tg}")
                W(f"bcastm_0_{tg}", "quant_47")
                TK(vector.tensor_copy(out=A_sc[:, tg * 512:tg * 512 + 512], in_=bank(2)),
                   "vp", 1, f"mb_{tg}")
                W(f"bcastr_0_{tg}")
                TK(vector.tensor_copy(out=A_sc[:, 1024 + tg * 512:1024 + tg * 512 + 512],
                                      in_=bank(3)), "vp", 1, f"rb_{tg}")
            for ec in range(EC):
                x_ec = xs(ec, 0, 1024)
                TK(vector.tensor_sub(x_ec, x_ec, A_sc[:, 0:1024]), "vp", 1)
                TK(vector.tensor_mul(x_ec, x_ec, A_sc[:, 1024:2048]), "vp", 1)
                TK(vector.tensor_copy(out=xbs(ec, 0, 1024), in_=x_ec),
                   "vp", 1, "x2done" if ec == EC - 1 else f"ln1_{ec}")
            if _PHASE < 5:
                return
            for tg in range(2):
                for hc in range(HC):
                    hid = (A_q if hc < 32 else A_exp)[:, (hc % 32) * 512:(hc % 32) * 512 + 512]
                    W(f"silu_{tg}_{hc}", f"gmm_{tg}_{hc}")
                    TK(vector.scalar_tensor_tensor(
                        out=hid, in0=bank(2 + hc % 2), scalar=scol(1),
                        in1=A_sc[:, (hc % 2) * 512:(hc % 2) * 512 + 512],
                        op0=ALU.mult, op1=ALU.mult), "vp", 1, f"hid_{tg}_{hc}")
                for oc in range(EC):
                    W(f"downmm_{tg}_{oc}_1")
                    TK(vector.scalar_tensor_tensor(
                        out=xs(oc, tg * 512, 512), in0=bank(4 + oc % 2),
                        scalar=scol(8), in1=xs(oc, tg * 512, 512),
                        op0=ALU.mult, op1=ALU.add), "vp", 1, f"x3_{tg}_{oc}")
                W(f"mrow_1_{tg}", f"sqrow_1_{tg}")
                TK(vector.tensor_mul(rowC, rowA, rowA), "vp", 1)
                TK(vector.tensor_sub(rowB, rowB, rowC), "vp", 1, f"var_1_{tg}")
                W(f"sqrt_1_{tg}")
                TK(vector.reciprocal(out=rowC, in_=rowC), "vp", 1, f"rrow_1_{tg}")
                TK(vector.tensor_mul(rowA, rowA, rowC), "vp", 1)
                TK(vector.tensor_scalar_mul(rowA, rowA, -1.0), "vp", 1, f"nmr_{tg}")
                for tsub in range(4):
                    W(f"colmm_{tg}_{tsub}")
                    TK(vector.tensor_copy(out=ln2col(tsub), in_=bank(6)[:, tsub:tsub + 1]),
                       "vp", 1)
                    TK(vector.tensor_copy(out=ln2col(4 + tsub), in_=bank(7)[:, tsub:tsub + 1]),
                       "vp", 1, f"cols_{tg}_{tsub}")
                for tsub in range(4):
                    grp = tsub % 2
                    W(f"trans_{tg}_{tsub}", f"ostore_{tg}_{tsub - 1}")
                    TK(vector.tensor_scalar(
                        out=A_sc[:, 0:2048], in0=pAll[:, grp * 2048:grp * 2048 + 2048],
                        scalar1=ln2col(tsub), scalar2=ln2col(4 + tsub),
                        op0=ALU.mult, op1=ALU.add), "vp", 1, f"ln2evac_{tg}_{tsub}")

        @block.tensor
        def _(tensor):
            W = lambda *e: C.wait(tensor, "tensor", *e)
            TK = lambda i, sem, n, ev=None: C.tick(i, sem, n, ev)
            W("ones", "xbload", "biasload", "p0done")
            wbuf = lambda b: A_wt[:, b * 4096:(b + 1) * 4096]
            for s, co, tg, g in kq_groups(8):
                W(f"kvw_{s}", f"kevac_{g - 2}")
                for ec in range(EC):
                    tensor.matmul(bank(g % 2),
                                  wbuf(s % 2)[:, ec * 256 + (co - 2 * s) * 128:
                                              ec * 256 + (co - 2 * s) * 128 + 128],
                                  xbs(ec, tg * 512, 512),
                                  start=(ec == 0), stop=False)
                TK(tensor.matmul(bank(g % 2), kb_b(co * 128, 128), ones512_b,
                                 start=False, stop=True),
                   "tp", 1, f"kproj_{g}")
                if co == 2 * s + 1 and tg == 1:
                    C.ev(f"kvslab_done_{s}", "tp")
                W(f"kevac_{g}", f"kt2_{g - 2}")
                TK(tensor.matmul(bank(2 + g % 2), rmatsl,
                                 A_sc[:, (g % 2) * 512:(g % 2) * 512 + 512],
                                 start=True, stop=True),
                   "tp", 1, f"rotmm_K_{g}")
            for vg in range(64):
                s, tc = vg // 8, vg % 8
                W(f"kvw_{8 + s}", f"vevac_{vg - 2}")
                for ec in range(EC):
                    tensor.matmul(bank(vg % 2, 256),
                                  xbs(ec, tc * 128, 128),
                                  wbuf((8 + s) % 2)[:, ec * 256:ec * 256 + 256],
                                  start=(ec == 0), stop=False)
                TK(tensor.matmul(bank(vg % 2, 256), ones128r_b, vb_b(s * 256, 256),
                                 start=False, stop=True),
                   "tp", 1, f"vproj_{vg}")
                if tc == 7:
                    C.ev(f"vslab_done_{8 + s}", "tp")
            for s, co, tg, g in kq_groups(8):
                W(f"qw_{s}", f"qevac_{g - 2}")
                if g < 2:
                    W(f"vevac_{62 + g}")
                for ec in range(EC):
                    tensor.matmul(bank(g % 2),
                                  wbuf(s % 2)[:, ec * 256 + (co - 2 * s) * 128:
                                              ec * 256 + (co - 2 * s) * 128 + 128],
                                  xbs(ec, tg * 512, 512),
                                  start=(ec == 0), stop=False)
                TK(tensor.matmul(bank(g % 2), qb_b(co * 128, 128), ones512_b,
                                 start=False, stop=True),
                   "tp", 1, f"qproj_{g}")
                if co == 2 * s + 1 and tg == 1:
                    C.ev(f"qslab_done_{s}", "tp")
                W(f"qevac_{g}", f"qt2_{g - 2}")
                TK(tensor.matmul(bank(2 + g % 2), rmatsl,
                                 A_sc[:, (g % 2) * 512:(g % 2) * 512 + 512],
                                 start=True, stop=True),
                   "tp", 1, f"rotmm_Q_{g}")
                if g == 31:
                    C.ev("qlastmm", "tp")
            if _PHASE < 2:
                return
            W("p1red")
            TK(tensor.matmul(bank(3)[0:1, 0:3], onescol_f,
                             s_acc[:, 48:51], start=True, stop=True),
               "tp", 1, "sclsum")
            W("sclload", "sclrow")
            TK(tensor.matmul(bank(3)[:, 0:3], ones128r_f, sclback_row[0:1, 0:3],
                             start=True, stop=True), "tp", 1, "thrbcast")
            if _PHASE < 3:
                return
            for qg in range(2 if _PHASE >= 3.9 else 1):
                for ks in range(1 if _PHASE == 3.06 else (16 if _PHASE >= 3.05 else 0)):
                    for tsub in range(2):
                        tc = 2 * ks + tsub
                        if _PHASE == 3.067:
                            W(f"kload_{qg}_{ks}")
                        else:
                            W(f"kload_{qg}_{ks}", f"exp_{qg}_{tc - 2}")
                        for ec in range(EC):
                            mm = tensor.matmul(
                                bank(tc % 2),
                                A_xb[:, (ks % 2) * 4096 + ec * 256 + tsub * 128:
                                     (ks % 2) * 4096 + ec * 256 + tsub * 128 + 128],
                                qslice(ec, qg * 512, 512),
                                start=(ec == 0), stop=(ec == EC - 1))
                        TK(mm, "tp", 1, f"score_{qg}_{tc}")
                        if tsub == 1:
                            C.ev(f"scoreslab_{qg}_{ks}", "tp")
                        if tc >= 1 and _PHASE >= 3.09:
                            W(f"exp_{qg}_{tc - 1}")
                            TK(tensor.matmul(bank(2)[0:1, :], onescol_b,
                                             A_exp[:, (tc - 1) * 512:tc * 512],
                                             start=(tc - 1 == 0), stop=False),
                               "tp", 1, f"summm_{qg}_{tc - 1}")
                if _PHASE >= 3.09:
                    W(f"exp_{qg}_{TC - 1}")
                    TK(tensor.matmul(bank(2)[0:1, :], onescol_b,
                                     A_exp[:, (TC - 1) * 512:TC * 512],
                                     start=False, stop=True),
                       "tp", 1, f"summm_{qg}_{TC - 1}")
                if _PHASE < 3.2:
                    continue
                W(f"recip_{qg}")
                TK(tensor.matmul(bank(0), ones128r_f, rowA, start=True, stop=True),
                   "tp", 1, f"recipbcmm_{qg}")
                for eh in range(2):
                    for tc in range(TC):
                        vs = tc // 2
                        W(f"vload_{qg}_{eh}_{vs}")
                        if tc == 0:
                            W(f"recB_{qg}", f"sumevac_{qg}")
                            if eh == 1:
                                for e8 in range(8):
                                    W(f"avevac_{qg}_0_{e8}")
                        for e8 in range(8):
                            mm = tensor.matmul(
                                bank(e8),
                                A_kv[:, (vs % 2) * 2048 + (tc % 2) * 1024 + e8 * 128:
                                     (vs % 2) * 2048 + (tc % 2) * 1024 + e8 * 128 + 128],
                                A_exp[:, tc * 512:(tc + 1) * 512],
                                start=(tc == 0), stop=(tc == TC - 1))
                        if tc % 2 == 1:
                            TK(mm, "tp", 1, f"avslab_{qg}_{eh}_{vs}")
                    C.ev(f"avdone_{qg}_{eh}", "tp")
                for oc in range(EC if _PHASE >= 3.3 else 0):
                    W(f"owload_{qg}_{oc // 2}", f"oadd_{qg}_{oc - 2}")
                    if oc == 0:
                        for e8 in range(8):
                            W(f"avevac_{qg}_1_{e8}")
                    for ec in range(EC):
                        mm = tensor.matmul(
                            bank(oc % 2),
                            wbuf((oc // 2) % 2)[:, ec * 256 + (oc % 2) * 128:
                                                ec * 256 + (oc % 2) * 128 + 128],
                            A_xb[:, 8192 + ec * 512:8192 + ec * 512 + 512],
                            start=(ec == 0), stop=(ec == EC - 1))
                    TK(mm, "tp", 1, f"oproj_{qg}_{oc}")
                    if oc % 2 == 1:
                        C.ev(f"oproj_slabdone_{qg}_{oc // 2}", "tp")
            if _PHASE < 4:
                return
            W("oadd_1_15")
            for tg in range(2):
                if tg == 1:
                    W("mb_0", "rb_0")
                for ec in range(EC):
                    mm = tensor.matmul(bank(2)[0:1, :], onescol_f, xs(ec, tg * 512, 512),
                                       start=(ec == 0), stop=(ec == EC - 1))
                TK(mm, "tp", 1, f"meanmm_0_{tg}")
                for ec in range(EC):
                    W(f"sq_0_{tg}_{ec}")
                    TK(tensor.matmul(bank(3)[0:1, :], onescol_b,
                                     A_rp[:, (ec % 2) * 512:(ec % 2) * 512 + 512],
                                     start=(ec == 0), stop=(ec == EC - 1)),
                       "tp", 1, f"sqmm_0_{tg}_{ec}")
                W(f"mrow_0_{tg}")
                TK(tensor.matmul(bank(2), ones128r_f, rowA, start=True, stop=True),
                   "tp", 1, f"bcastm_0_{tg}")
                W(f"rrow_0_{tg}")
                TK(tensor.matmul(bank(3), ones128r_f, rowC, start=True, stop=True),
                   "tp", 1, f"bcastr_0_{tg}")
            if _PHASE < 5:
                return
            for tg in range(2):
                W("x2done")
                if tg == 1:
                    W("ln2evac_0_2", "ln2evac_0_3", "cols_0_3")
                for j in range(32):
                    for s01 in range(2):
                        hc = 2 * j + s01
                        W(f"uload_{tg}_{j}", f"silu_{tg}_{hc - 2}")
                        for ec in range(EC):
                            mm = tensor.matmul(
                                bank(hc % 2),
                                A_wt[:, (ec // 2) * 512 + (ec % 2) * 256 + s01 * 128:
                                     (ec // 2) * 512 + (ec % 2) * 256 + s01 * 128 + 128],
                                xbs(ec, tg * 512, 512),
                                start=(ec == 0), stop=(ec == EC - 1))
                        TK(mm, "tp", 1, f"upg_{tg}_{hc}")
                        if s01 == 1:
                            C.ev(f"upmm_{tg}_{j}", "tp")
                        W(f"gload_{tg}_{j}", f"hid_{tg}_{hc - 2}")
                        for ec in range(EC):
                            mm = tensor.matmul(
                                bank(2 + hc % 2),
                                A_wt[:, 4096 + (ec // 2) * 512 + (ec % 2) * 256 + s01 * 128:
                                     4096 + (ec // 2) * 512 + (ec % 2) * 256 + s01 * 128 + 128],
                                xbs(ec, tg * 512, 512),
                                start=(ec == 0), stop=(ec == EC - 1))
                        TK(mm, "tp", 1, f"gmm_{tg}_{hc}")
                        if s01 == 1:
                            C.ev(f"gatemm_{tg}_{j}", "tp")
                W(f"hid_{tg}_{HC - 1}")
                for oc in range(EC):
                    W(f"x3_{tg}_{oc - 2}")
                    for hh in range(2):
                        W(f"dload_{tg}_{oc}_{hh}")
                        for i in range(32):
                            hc = hh * 32 + i
                            hid = (A_q if hc < 32 else A_exp)[
                                :, (hc % 32) * 512:(hc % 32) * 512 + 512]
                            mm = tensor.matmul(
                                bank(4 + oc % 2),
                                A_wt[:, (oc * 2 + hh) % 2 * 4096 + i * 128:
                                     (oc * 2 + hh) % 2 * 4096 + i * 128 + 128],
                                hid,
                                start=(hc == 0), stop=(hc == HC - 1))
                        TK(mm, "tp", 1, f"downmm_{tg}_{oc}_{hh}")
                W(f"x3_{tg}_{EC - 1}")
                for ec in range(EC):
                    mm = tensor.matmul(bank(6)[0:1, :], onescol_f, xs(ec, tg * 512, 512),
                                       start=(ec == 0), stop=(ec == EC - 1))
                TK(mm, "tp", 1, f"meanmm_1_{tg}")
                for ec in range(EC):
                    W(f"sq_1_{tg}_{ec}")
                    TK(tensor.matmul(bank(7)[0:1, :], onescol_b,
                                     A_rp[:, (ec % 2) * 512:(ec % 2) * 512 + 512],
                                     start=(ec == 0), stop=(ec == EC - 1)),
                       "tp", 1, f"sqmm_1_{tg}_{ec}")
                W(f"rrow_1_{tg}", f"nmr_{tg}")
                for tsub in range(4):
                    TK(tensor.matmul(bank(6)[:, tsub:tsub + 1],
                                     rowC[0:1, tsub * 128:tsub * 128 + 128], ones1_f,
                                     start=True, stop=True), "tp", 1)
                    TK(tensor.matmul(bank(7)[:, tsub:tsub + 1],
                                     rowA[0:1, tsub * 128:tsub * 128 + 128], ones1_f,
                                     start=True, stop=True), "tp", 1, f"colmm_{tg}_{tsub}")
                for tsub in range(4):
                    grp = tsub % 2
                    W(f"ln2evac_{tg}_{tsub - 2}", f"cols_{tg}_3")
                    for ec in range(EC):
                        mm = tensor.transpose(
                            pAll[:, grp * 2048 + ec * 128:grp * 2048 + ec * 128 + 128],
                            xs(ec, tg * 512 + tsub * 128, 128), identsl)
                    TK(mm, "tp", 1, f"trans_{tg}_{tsub}")

        @block.gpsimd
        def _(gpsimd):
            W = lambda *e: C.wait(gpsimd, "gpsimd", *e)
            TK = lambda i, sem, n, ev=None: C.tick(i, sem, n, ev)
            TK(gpsimd.dma_start(out=warm_a, in_=qb[0:1, 0:4]), "qR", 16)
            C.wait_total(gpsimd, "gpsimd", "qR")
            TK(gpsimd.collective_compute("AllGather", ALU.bypass,
                                         replica_groups=ALL_GROUP,
                                         ins=[warm_a.opt()], outs=[warm_b.opt()]),
               "cc0", 1, "cc_warm")
            TK(gpsimd.dma_start(
                out=A_xb.ap().rearrange("p (ec t) -> p ec t", t=1024),
                in_=xT.rearrange("(ec p) t -> p ec t", p=128)), "qXB", 16, "xbload")
            TK(gpsimd.dma_start(out=A_exp[0:1, 0:2048], in_=qb), "qR", 16)
            TK(gpsimd.dma_start(out=A_exp[0:1, 2048:4096], in_=kb), "qR", 16)
            TK(gpsimd.dma_start(out=A_exp[0:1, 4096:6144], in_=vb), "qR", 16, "biasload")

            def slab_dma(dst_buf, src, c0, ev):
                TK(gpsimd.dma_start(
                    out=A_wt.ap()[:, dst_buf * 4096:dst_buf * 4096 + 4096]
                        .rearrange("p (ec c) -> p ec c", c=256),
                    in_=src[:, c0:c0 + 256].rearrange("(ec p) c -> p ec c", p=128)),
                   f"qW{dst_buf}", 16, ev)

            slab_src = [(kvwT, s * 256, f"kvw_{s}") for s in range(16)] + \
                       [(qwT, s * 256, f"qw_{s}") for s in range(8)]
            done_ev = [f"kvslab_done_{s}" for s in range(8)] + \
                      [f"vslab_done_{s}" for s in range(8, 16)] + \
                      [f"qslab_done_{s}" for s in range(8)]
            kst = [0]

            def kstores(upto):
                while kst[0] < min(upto, 32):
                    g = kst[0]
                    kst[0] += 1
                    s2, co = g // 4, 2 * (g // 4) + (g % 4) // 2
                    tg = g % 2
                    W(f"krope_{g}")
                    TK(gpsimd.dma_start(
                        out=k_send[co * 128:(co + 1) * 128, tg * 512:(tg + 1) * 512],
                        in_=A_kv[:, (g % 2) * 512:(g % 2) * 512 + 512]),
                       f"qKS{g % 2}", 16, f"kstore_{g}")

            def tile_in_g(i):
                r, jj = i // 16, i % 16
                if r < 2:
                    srcw = (upT_s, gateT_s)[r]
                    rr, cc2 = jj // 8, jj % 8
                    return srcw[rr * 128:(rr + 1) * 128, cc2 * 1024:(cc2 + 1) * 1024]
                return downT_s[jj]

            p1g = [0]

            def p1loads(n):
                for _ in range(n):
                    if _PHASE < 2 or p1g[0] >= 48:
                        return
                    i = p1g[0]
                    p1g[0] += 1
                    W(f"p1_{i - 8}")
                    TK(gpsimd.dma_start(
                        out=A_exp[:, 6656 + (i % 8) * 1024:6656 + (i % 8) * 1024 + 1024],
                        in_=tile_in_g(i)), f"qPA{i % 8}", 16, f"p1load_{i}")

            for j in range(16):
                if 2 <= j < 10:
                    kstores((j - 1) * 4)
                if j == 10:
                    kstores(32)
                    W("kstore_30", "kstore_31")
                    TK(gpsimd.collective_compute("AllGather", ALU.bypass,
                                                 replica_groups=KV_GROUPS,
                                                 ins=[k_send.opt()], outs=[k_full.opt()]),
                       "cc1", 1, "cc_k")
                if j >= 2:
                    W(done_ev[j - 2])
                src, c0, ev = slab_src[j]
                slab_dma(j % 2, src, c0, ev)
                p1loads(2)
            kstores(32)
            for j in range(16, 24):
                W(done_ev[j - 2])
                src, c0, ev = slab_src[j]
                slab_dma(j % 2, src, c0, ev)
                p1loads(1)
            W("vstore_62", "vstore_63")
            TK(gpsimd.collective_compute("AllGather", ALU.bypass,
                                         replica_groups=KV_GROUPS,
                                         ins=[v_send.opt()], outs=[v_full.opt()]),
               "cc2", 1, "cc_v")
            p1loads(48)
            if _PHASE >= 2:
                W("sclstore")
                TK(gpsimd.collective_compute("AllReduce", ALU.add,
                                             replica_groups=ALL_GROUP,
                                             ins=[scl_send.opt()], outs=[scl_full.opt()]),
                   "cc3", 1, "cc_scl")
            if _PHASE >= 2:
                W("qtstore_15")
                TK(gpsimd.collective_compute("AllGather", ALU.bypass,
                                             replica_groups=ALL_GROUP,
                                             ins=[up_send.opt()], outs=[up_full.opt()]),
                   "cc4", 1, "cc_up")
                W("qtstore_31")
                TK(gpsimd.collective_compute("AllGather", ALU.bypass,
                                             replica_groups=ALL_GROUP,
                                             ins=[gate_send.opt()], outs=[gate_full.opt()]),
                   "cc5", 1, "cc_gate")
                W("qtstore_47")
                TK(gpsimd.collective_compute("AllGather", ALU.bypass,
                                             replica_groups=ALL_GROUP,
                                             ins=[down_send.opt()], outs=[down_full.opt()]),
                   "cc6", 1, "cc_down")
            if _PHASE >= 3.3:
                for qg in range(2 if _PHASE >= 3.9 else 1):
                    for s in range(8):
                        W(f"oproj_slabdone_{qg}_{s - 2}" if s >= 2 else
                          (f"oproj_slabdone_{qg - 1}_6" if qg == 1 else "qslab_done_7"))
                        if qg == 1 and s == 1:
                            W("oproj_slabdone_0_7")
                        slab_dma(s % 2, owT, s * 256, f"owload_{qg}_{s}")
            if _PHASE == 3.06:
                W("dbg2")
                TK(gpsimd.dma_start(out=out_s[0:128, 0:1024], in_=A_sc[:, 0:1024]),
                   "qOS", 16, "dbgstore")
            if _PHASE in (3.95, 4.5):
                W("oadd_1_15" if _PHASE == 3.95 else "x2done")
                TK(gpsimd.dma_start(out=out_s[0:128, :], in_=A_x[:, 0:2048]),
                   "qOS", 16, "dbgx")
            if _PHASE == 2.5:
                W("cc_up")
                TK(gpsimd.dma_start(out=out_s[0:128, :],
                                    in_=up_full[0, 0, :, 0:2048]), "qOS", 16, "dbgu")
            for tg in range(2 if (_PHASE >= 5 and _PHASE != 5.5) else 0):
                for tsub in range(4):
                    W(f"ln2evac_{tg}_{tsub}")
                    TK(gpsimd.dma_start(
                        out=out_s[tg * 512 + tsub * 128:tg * 512 + tsub * 128 + 128, :],
                        in_=A_sc[:, 0:2048]), "qOS", 16, f"ostore_{tg}_{tsub}")
            if _PHASE == 5.5:
                W("x3_1_15")
                TK(gpsimd.dma_start(out=out_s[0:128, :], in_=A_x[:, 0:2048]),
                   "qOS", 16, "dbgx3")
            if _PHASE == 5.6:
                W("x3_1_15")
                TK(gpsimd.dma_start(out=out_dbg, in_=A_x[:, :]), "qOS", 16, "dbgx3f")
            if _PHASE == 5.7:
                W("x3_1_15")
                TK(gpsimd.dma_start(out=out_dbg[:, 0:2048], in_=s_mf[:, :]),
                   "qOS", 16, "dbg57a")
                TK(gpsimd.dma_start(out=out_dbg[:, 2048:4096], in_=A_q[:, 0:2048]),
                   "qOS", 16, "dbg57b")
                TK(gpsimd.dma_start(out=out_dbg[:, 4096:6144], in_=A_exp[:, 0:2048]),
                   "qOS", 16, "dbg57c")
                TK(gpsimd.dma_start(out=out_dbg[:, 6144:8192], in_=A_xb[:, 0:2048]),
                   "qOS", 16, "dbg57d")
            C.wait_total(gpsimd, "gpsimd", "qOS")

    return nc, C.events


_CACHE = {}


def _get_nc():
    if "nc" not in _CACHE:
        _, events = _build(None)
        nc, events2 = _build(events)
        _CACHE["nc"] = nc
    return _CACHE["nc"]


def _rope_tables():
    half = 64
    inv = 1.0 / (10000.0 ** (np.arange(half, dtype=np.float64) / half))
    pos = np.arange(S, dtype=np.float64)
    ang = pos[None, :] * inv[:, None]
    ang = np.concatenate([ang, ang], axis=0)
    cos, sin = np.cos(ang), np.sin(ang)
    out = []
    for t64 in (cos, sin):
        hi = t64.astype(np.float32)
        lo = (t64 - hi.astype(np.float64)).astype(np.float32)
        out.append((hi, lo))
    return out


def kernel(L0_attn_x, L0_attn_q_w, L0_attn_q_b, L0_attn_kv_w, L0_attn_kv_b,
           L0_attn_o_w, L0_ffn_W_up, L0_ffn_W_gate, L0_ffn_W_down):
    f32 = np.float32
    x = np.asarray(L0_attn_x, f32)
    qwT = np.ascontiguousarray(np.asarray(L0_attn_q_w, f32).T)
    kvwT = np.ascontiguousarray(np.asarray(L0_attn_kv_w, f32).T)
    owT = np.ascontiguousarray(np.asarray(L0_attn_o_w, f32).T)
    qb = np.ascontiguousarray(np.asarray(L0_attn_q_b, f32).reshape(1, E))
    kvb = np.asarray(L0_attn_kv_b, f32)
    kb = np.ascontiguousarray(kvb[:E].reshape(1, E))
    vb = np.ascontiguousarray(kvb[E:].reshape(1, E))
    upT = np.ascontiguousarray(np.asarray(L0_ffn_W_up, f32).T)
    gateT = np.ascontiguousarray(np.asarray(L0_ffn_W_gate, f32).T)
    downT = np.ascontiguousarray(np.asarray(L0_ffn_W_down, f32).T)

    (cos_hi, cos_lo), (sin_hi, sin_lo) = _rope_tables()
    R = np.zeros((128, 128), f32)
    for d in range(64):
        R[d + 64, d] = -1.0
        R[d, d + 64] = 1.0
    ident = np.eye(128, dtype=f32)

    nc = _get_nc()
    in_maps = []
    for c in range(NCORE):
        b, q4 = c // 4, c % 4
        t0 = q4 * T
        xT_c = np.ascontiguousarray(x[b, t0:t0 + T, :].T)
        down_sl = downT[c * 1024:(c + 1) * 1024, :]
        down_tiled = np.ascontiguousarray(
            down_sl.reshape(8, 128, 16, 128).transpose(2, 1, 0, 3).reshape(16, 128, 1024))
        in_maps.append(dict(
            xT=xT_c, qwT=qwT, kvwT=kvwT, owT=owT, qb=qb, kb=kb, vb=vb,
            upT_s=np.ascontiguousarray(upT[c * 256:(c + 1) * 256, :]),
            gateT_s=np.ascontiguousarray(gateT[c * 256:(c + 1) * 256, :]),
            downT_s=down_tiled,
            cos_hi=np.ascontiguousarray(cos_hi[:, t0:t0 + T]),
            cos_lo=np.ascontiguousarray(cos_lo[:, t0:t0 + T]),
            sin_hi=np.ascontiguousarray(sin_hi[:, t0:t0 + T]),
            sin_lo=np.ascontiguousarray(sin_lo[:, t0:t0 + T]),
            rmat=R, ident=ident))

    res = run_bass_kernel_spmd(nc, in_maps, core_ids=list(range(NCORE)))
    out = np.empty((B, S, E), f32)
    for c in range(NCORE):
        out[c // 4, (c % 4) * T:(c % 4 + 1) * T, :] = res.results[c]["out"]
    return out
